# version 16
# baseline (speedup 1.0000x reference)
import numpy as np
import ml_dtypes
import concourse.bass as bass
import concourse.mybir as mybir
from concourse.bass_utils import run_bass_kernel_spmd
from contextlib import ExitStack

F32 = mybir.dt.float32
BF16 = mybir.dt.bfloat16
AF = mybir.ActivationFunctionType
ALU = mybir.AluOpType
AX = mybir.AxisListType

P = 128
T = 4096
E = 1024
NPASS = 8
CPP = 4
NEG = -1e9

RH, DK, DV = 4, 256, 512
GAMMAS = [1.0 - 2.0 ** (-5.0 - h) for h in range(RH)]
FH, DH = 8, 64
HID = 2730
MSTR = 22
WIN = 512
QKS = 8.0

ENGINES = ("pe", "act", "dve", "sync", "gps")
DMA_ENG = {"sync", "gps"}


class Rec:
    def __init__(self):
        self.ops = {e: [] for e in ENGINES}
        self.all = []
        self.last_writer = {}
        self.readers = {}
        self.bar_arr = []
        self.bar_wait = []
        self.seg = 0

    def add(self, eng, fn, reads=(), writes=(), tag=""):
        pos = len(self.ops[eng])
        deps = {}
        rawdeps = []
        for b in tuple(reads) + tuple(writes):
            w = self.last_writer.get(b)
            if w is not None:
                deps[w[0]] = max(deps.get(w[0], -1), w[1])
                rawdeps.append(w)
        for b in writes:
            for r in self.readers.get(b, ()):
                deps[r[0]] = max(deps.get(r[0], -1), r[1])
                rawdeps.append(r)
        o = dict(eng=eng, fn=fn, deps=deps, inc=0, waits=[], tag=tag, pos=pos,
                 force_inc=False, is_barw=None, seg=self.seg,
                 dep_ops=[self.ops[e2][p2] for (e2, p2) in set(rawdeps)])
        self.ops[eng].append(o)
        self.all.append(o)
        for b in reads:
            self.readers.setdefault(b, []).append((eng, pos))
        for b in writes:
            self.last_writer[b] = (eng, pos)
            self.readers[b] = []
        return o

    def fence(self, reads, writes, tag="fence"):
        self.add("dve", lambda v: v.engine_nop(), reads=reads, writes=writes, tag=tag)

    def barrier(self, tag="bar"):
        self.seg += 1
        bid = len(self.bar_arr)
        arr = {"__seg__": self.seg}
        for e in ENGINES:
            if e in DMA_ENG:
                continue
            o = self.add(e, lambda q: q.drain(), [], [], tag=f"{tag}{bid}a_{e}")
            o["force_inc"] = True
            arr[e] = o
        self.bar_arr.append(arr)
        for e in ENGINES:
            if e in DMA_ENG:
                continue
            o = self.add(e, lambda q: q.nop(), [], [], tag=f"{tag}{bid}w_{e}")
            o["is_barw"] = bid
            self.bar_wait.append(o)
        self.seg += 1

    COST = {"pe": 400, "act": 600, "dve": 600, "sync": 1500, "gps": 2500}
    BIGPE = ("qkp", "vg", "wo", "w1", "w2", "qkv", "fo")
    TRPE = ("xT", "x1T", "hT", "qT", "kT", "ktr", "oT", "aT", "bT", "wT", "rT")

    def opcost(self, o):
        t = o["tag"]
        if o["eng"] == "pe":
            if t.startswith(("wo", "w2")):
                return 3800
            if t.startswith(self.BIGPE):
                return 2000
            if t.startswith(self.TRPE):
                return 1200
            return 500
        if o["eng"] in DMA_ENG:
            if t.startswith(("w1", "w2l", "wqk", "wvg", "wwo", "wqkv", "wfo")):
                return 2800
            if t.startswith(("xload", "x1sp", "ost", "x1rl")):
                return 4000
            return 1500
        if o["eng"] == "dve":
            if t.startswith(("og", "or", "r1", "r2", "qe", "ke", "xE", "x1E", "oE",
                             "woe", "foe", "bre", "wre", "res", "vge", "su")):
                return 900
            return 350
        return self.COST[o["eng"]]

    def schedule(self):
        import heapq
        for o in self.all:
            o["nsucc"] = 0
            o["cp"] = self.opcost(o)
        for o in self.all:
            for d in o["dep_ops"]:
                d["nsucc"] += 1
        for o in reversed(self.all):
            for d in o["dep_ops"]:
                c = o["cp"] + self.opcost(d)
                if c > d["cp"]:
                    d["cp"] = c
        neworder = []
        i = 0
        allops = self.all
        n = len(allops)
        for o in allops:
            o["unmet"] = len(o["dep_ops"])
            o["succ"] = []
        for o in allops:
            for d in o["dep_ops"]:
                d["succ"].append(o)
        gidx = {id(o): k for k, o in enumerate(allops)}
        while i < n:
            seg = allops[i]["seg"]
            j = i
            while j < n and allops[j]["seg"] == seg:
                j += 1
            segops = allops[i:j]
            if seg % 2 == 1:
                neworder.extend(segops)
                for o in segops:
                    for s in o["succ"]:
                        s["unmet"] -= 1
            else:
                ready = []
                for o in segops:
                    if o["unmet"] == 0:
                        heapq.heappush(ready, (-o["cp"], gidx[id(o)], id(o), o))
                insg = set(id(o) for o in segops)
                done = 0
                byid = {id(o): o for o in segops}
                while done < len(segops):
                    assert ready, "scheduler stuck"
                    _, _, _, o = heapq.heappop(ready)
                    neworder.append(o)
                    done += 1
                    for s in o["succ"]:
                        s["unmet"] -= 1
                        if s["unmet"] == 0 and id(s) in insg:
                            heapq.heappush(ready, (-s["cp"], gidx[id(s)], id(s), s))
            i = j
        self.all = neworder
        self.ops = {e: [] for e in ENGINES}
        for o in self.all:
            o["pos"] = len(self.ops[o["eng"]])
            self.ops[o["eng"]].append(o)
        for o in self.all:
            deps = {}
            for d in o["dep_ops"]:
                deps[d["eng"]] = max(deps.get(d["eng"], -1), d["pos"])
            o["deps"] = deps
        self.finalize()

    def finalize(self):
        needed = {e: set() for e in ENGINES}
        for op in self.all:
            for (pe_, pp) in op["deps"].items():
                if pe_ == "pe" and op["eng"] == "pe":
                    continue
                needed[pe_].add(pp)
        self.cum = {}
        for e in ENGINES:
            cnt = 0
            step = 16 if e in DMA_ENG else 1
            for i, op in enumerate(self.ops[e]):
                if op["is_barw"] is None and (e in DMA_ENG or i in needed[e] or op["force_inc"]):
                    op["inc"] = step
                    cnt += step
                self.cum[(e, i)] = cnt
        for op in self.all:
            byeng = {}
            for (pe_, pp) in op["deps"].items():
                if pe_ == "pe" and op["eng"] == "pe":
                    continue
                ops_p = self.ops[pe_]
                j = pp
                while j < len(ops_p) and ops_p[j]["inc"] == 0:
                    j += 1
                assert j < len(ops_p), f"no inc on {pe_} after {pp} for {op['tag']}"
                byeng[pe_] = max(byeng.get(pe_, 0), self.cum[(pe_, j)])
            if op["eng"] in DMA_ENG and op["pos"] > 0:
                prev = self.cum[(op["eng"], op["pos"] - 1)]
                byeng[op["eng"]] = max(byeng.get(op["eng"], 0), prev)
            if op["is_barw"] is not None:
                arr = self.bar_arr[op["is_barw"]]
                barseg = arr["__seg__"]
                for e2 in ENGINES:
                    if e2 == op["eng"]:
                        continue
                    if e2 in DMA_ENG:
                        last = -1
                        for k, o2 in enumerate(self.ops[e2]):
                            if o2["seg"] < barseg:
                                last = k
                            else:
                                break
                        if last >= 0:
                            byeng[e2] = max(byeng.get(e2, 0), self.cum[(e2, last)])
                    else:
                        byeng[e2] = max(byeng.get(e2, 0),
                                        self.cum[(e2, arr[e2]["pos"])])
            op["waits"] = sorted(byeng.items())

    def emit(self, sems, engmap):
        self.schedule()
        for e in ENGINES:
            if not self.ops[e]:
                continue

            def body(eng, e=e):
                for op in self.ops[e]:
                    for (pe_, val) in op["waits"]:
                        eng.wait_ge(sems[pe_], val)
                    inst = op["fn"](eng)
                    if op["inc"]:
                        assert inst is not None, f"{e}:{op['tag']} returned None"
                        if e in ("dve", "act", "pe"):
                            eng.drain().then_inc(sems[e], op["inc"])
                        else:
                            inst.then_inc(sems[e], op["inc"])
            engmap[e](body)


def host_constants():
    c = {}
    c["ident"] = np.eye(P, dtype=np.float32)
    mtil = np.zeros((P, RH * P), np.float32)
    gct = np.zeros((P, RH), np.float32)
    gcks = np.zeros((P, RH), np.float32)
    for h, g in enumerate(GAMMAS):
        s = np.arange(P)
        mtil[:, h * P:(h + 1) * P] = np.where(
            np.arange(P)[None, :] >= s[:, None],
            np.power(g, -s.astype(np.float64))[:, None], 0.0)
        gct[:, h] = np.power(g, np.arange(P, dtype=np.float64))
        gcks[:, h] = np.power(g, P - np.arange(P, dtype=np.float64))
    c["ret_mtil"] = mtil
    c["ret_gcol_t"] = gct
    c["ret_gcol_ks"] = gcks
    j = np.arange(P, dtype=np.float64)
    inv = 1.0 / (10000.0 ** (2 * j / DK))
    fr = np.outer(inv, np.arange(T, dtype=np.float64))
    c["ret_rot4"] = np.stack([np.cos(fr) / 16.0, np.sin(fr) / 16.0,
                              np.cos(fr), np.sin(fr)], axis=1).astype(np.float32)
    hj = np.arange(0, DH, 2, dtype=np.float64)
    invf = 1.0 / (10000.0 ** (hj / DH))
    fr2 = np.outer(np.arange(T, dtype=np.float64), invf)
    base = (hj + 0.4 * DH) / (1.4 * DH)
    power = (np.arange(T, dtype=np.float64) - T // 2) / (WIN // 2)
    scale = base[None, :] ** power[:, None]
    c["ffn_ct"] = np.concatenate(
        [np.cos(fr2) * scale * QKS, np.sin(fr2) * scale * QKS,
         np.cos(fr2) / scale, np.sin(fr2) / scale], axis=1).astype(np.float32)
    sl = np.arange(P)[:, None]
    tl = np.arange(P)[None, :]
    c["tri_causal"] = np.where(sl <= tl, 0.0, NEG).astype(np.float32)
    c["tri_far"] = np.where(sl >= tl, 0.0, NEG).astype(np.float32)
    psh = np.zeros((P, P), np.float32)
    psh[np.arange(P - 1), np.arange(1, P)] = 1.0
    c["shift_dn"] = psh
    pshb = np.zeros((P, P), np.float32); pshb[P - 1, 0] = 1.0
    c["shift_dnb"] = pshb
    psu = np.zeros((P, P), np.float32)
    psu[np.arange(1, P), np.arange(P - 1)] = 1.0
    c["shift_up"] = psu
    psub = np.zeros((P, P), np.float32); psub[0, P - 1] = 1.0
    c["shift_upb"] = psub
    return c


def make_core_inputs(inp, core, cns):
    m = {}
    m["x"] = np.asarray(inp["L0_attn_x"][core], np.float32)
    wq = np.asarray(inp["L0_attn_Wq"], np.float32)
    wk = np.asarray(inp["L0_attn_Wk"], np.float32)
    wv = np.asarray(inp["L0_attn_Wv"], np.float32)
    wg = np.asarray(inp["L0_attn_Wg"], np.float32)
    wo = np.asarray(inp["L0_attn_Wo"], np.float32)
    m["wq_s"] = wq.reshape(E, 8, P).transpose(1, 0, 2).reshape(8 * E, P)
    m["wk_s"] = wk.reshape(E, 8, P).transpose(1, 0, 2).reshape(8 * E, P)
    m["wv_s"] = wv.reshape(E, 4, DV).transpose(1, 0, 2).reshape(4 * E, DV)
    m["wg_s"] = wg.reshape(E, 4, DV).transpose(1, 0, 2).reshape(4 * E, DV)
    m["wo_s"] = wo.reshape(2 * E, 8, P).transpose(1, 0, 2).reshape(8 * 2 * E, P)
    m["gn"] = np.asarray(inp["L0_attn_gn_weight"], np.float32).reshape(1, DV)
    m["fwqkv"] = np.asarray(inp["L0_ffn_wqkv"], np.float32).reshape(2 * E, 3 * FH * DH)
    m["fqs"] = np.asarray(inp["L0_ffn_q_scale"], np.float32)
    m["fks"] = np.asarray(inp["L0_ffn_k_scale"], np.float32)
    fwo = np.asarray(inp["L0_ffn_wo"], np.float32)
    m["fwo_s"] = fwo.reshape(2, FH * DH, 8, P).transpose(0, 2, 1, 3).reshape(2 * 8 * FH * DH, P)
    lng = np.asarray(inp["L0_ffn_ln_g"], np.float32)
    lnb = np.asarray(inp["L0_ffn_ln_b"], np.float32)
    m["flng_c"] = lng.reshape(2, 8, P).transpose(0, 2, 1).reshape(2 * P, 8)
    m["flnb_c"] = lnb.reshape(2, 8, P).transpose(0, 2, 1).reshape(2 * P, 8)
    w1 = np.asarray(inp["L0_ffn_w1"], np.float32)
    w1a = np.zeros((2, E, MSTR * P), np.float32); w1a[:, :, :HID] = w1[:, :, :HID]
    w1g = np.zeros((2, E, MSTR * P), np.float32); w1g[:, :, :HID] = w1[:, :, HID:]
    m["fw1a_s"] = w1a.reshape(2, E, MSTR, P).transpose(0, 2, 1, 3).reshape(2 * MSTR * E, P)
    m["fw1g_s"] = w1g.reshape(2, E, MSTR, P).transpose(0, 2, 1, 3).reshape(2 * MSTR * E, P)
    b1 = np.asarray(inp["L0_ffn_b1"], np.float32)
    b1a = np.zeros((2, MSTR * P), np.float32); b1a[:, :HID] = b1[:, :HID]
    b1g = np.zeros((2, MSTR * P), np.float32); b1g[:, :HID] = b1[:, HID:]
    m["fb1a_c"] = b1a.reshape(2, MSTR, P).transpose(0, 2, 1).reshape(2 * P, MSTR)
    m["fb1g_c"] = b1g.reshape(2, MSTR, P).transpose(0, 2, 1).reshape(2 * P, MSTR)
    w2 = np.asarray(inp["L0_ffn_w2"], np.float32)
    w2p = np.zeros((2, MSTR * P, E), np.float32); w2p[:, :HID, :] = w2
    m["fw2_s"] = w2p.reshape(2, MSTR * P, 8, P).transpose(0, 2, 1, 3).reshape(2 * 8 * MSTR * P, P)
    b2 = np.asarray(inp["L0_ffn_b2"], np.float32)
    m["fb2_c"] = b2.reshape(2, 8, P).transpose(0, 2, 1).reshape(2 * P, 8)
    for k, v in cns.items():
        m["c_" + k] = v
    return {k: np.ascontiguousarray(v) for k, v in m.items()}


def build_program(dumps=(), stages=("A", "B0", "B1", "F"), DBG_SKIP_LN1=False, DBG_HOST_W=False, BARRIERS=True):
    nc = bass.Bass("TRN2", target_bir_lowering=False)
    dumps = set(dumps)
    cns = host_constants()
    laststage = stages[-1]

    def din(name, shape):
        return nc.dram_tensor(name, list(shape), F32, kind="ExternalInput").ap()

    x_h = din("x", [T, E])
    wq_h = din("wq_s", [8 * E, P]); wk_h = din("wk_s", [8 * E, P])
    wv_h = din("wv_s", [4 * E, DV]); wg_h = din("wg_s", [4 * E, DV])
    wo_h = din("wo_s", [8 * 2 * E, P])
    gn_h = din("gn", [1, DV])
    fwqkv_h = din("fwqkv", [2 * E, 3 * FH * DH])
    fqs_h = din("fqs", [2, DH]); fks_h = din("fks", [2, DH])
    fwo_h = din("fwo_s", [2 * 8 * FH * DH, P])
    flng_h = din("flng_c", [2 * P, 8]); flnb_h = din("flnb_c", [2 * P, 8])
    fw1a_h = din("fw1a_s", [2 * MSTR * E, P]); fw1g_h = din("fw1g_s", [2 * MSTR * E, P])
    fb1a_h = din("fb1a_c", [2 * P, MSTR]); fb1g_h = din("fb1g_c", [2 * P, MSTR])
    fw2_h = din("fw2_s", [2 * 8 * MSTR * P, P])
    fb2_h = din("fb2_c", [2 * P, 8])
    dcn = {k: nc.dram_tensor("c_" + k, list(v.shape), F32, kind="ExternalInput").ap()
           for k, v in cns.items()}

    out_h = nc.dram_tensor("out", [T, E], F32, kind="ExternalOutput").ap()
    x1sp_h = nc.dram_tensor("x1spill", [T, E], F32, kind="Internal").ap()
    wsc = {}
    for nm, shape in [("wq", [8 * E, P]), ("wk", [8 * E, P]), ("wv", [4 * E, DV]),
                      ("wg", [4 * E, DV]), ("wo", [8 * 2 * E, P]),
                      ("wqkv", [2 * E, 3 * FH * DH]), ("fwo", [2 * 8 * FH * DH, P]),
                      ("w1a", [2 * MSTR * E, P]), ("w1g", [2 * MSTR * E, P]),
                      ("w2", [2 * 8 * MSTR * P, P])]:
        wsc[nm] = nc.dram_tensor("wsc_" + nm, shape, BF16, kind="Internal").ap()
    dump_shapes = {"qfm": [P, 8 * 512], "kfm": [P, 8 * 512], "vsb": [P, 2 * 2048],
                   "gsb": [P, 2 * 2048], "osb": [P, 2048], "otil": [P, 2048],
                   "ofm": [P, 16 * 256], "aofm": [P, 8 * 512], "x1fm": [P, 8 * 520],
                   "qkhat": [P, 8 * 130], "qhfm": [P, 8 * 512], "psb": [P, 8 * 512],
                   "oall": [P, 4 * 512], "prfm": [P, 8 * 512], "h2fm": [P, 8 * 520],
                   "actfm": [P, MSTR * 512], "kxfm": [P, 4 * FH * 512],
                   "vfsb": [P, 16 * FH * 66]}
    dump_h = {d: nc.dram_tensor("dump_" + d, dump_shapes[d], F32, kind="ExternalOutput").ap()
              for d in dumps}

    rec = Rec()

    with ExitStack() as st:
        def sb(name, shape, dt=F32):
            return st.enter_context(nc.sbuf_tensor(name, shape, dt))

        def pst(name, shape, dt=F32):
            return st.enter_context(nc.psum_tensor(name, shape, dt))

        strm = sb("strm", [P, CPP * E])

        AB_N = 35584
        ab = sb("ab", [P, AB_N], BF16)
        pos = [0]

        def carve(nel):
            o = pos[0]; pos[0] += nel
            assert pos[0] <= AB_N, f"arena overflow {pos[0]}"
            return ab[:, o:o + nel]

        a0 = pos[0]
        xfm = carve(8 * 512)
        qfm = carve(8 * 512)
        kfm = carve(8 * 512)
        vsb = carve(2 * 2048)
        gsb = carve(2 * 2048)
        otil = carve(2048)
        ofm = carve(16 * 256)
        aofm = carve(8 * 512)
        A_END = pos[0]
        pos[0] = a0
        x1fm = carve(8 * 520)
        qhfm = carve(8 * 512)
        psb = carve(8 * 512)
        qkhat = carve(8 * 130)
        h2fm = carve(8 * 520)
        actfm = carve(MSTR * 512)
        prfm = carve(8 * 512)
        oall = carve(4 * 512)
        rbc = carve(512)
        B_END = pos[0]
        assert max(A_END, B_END) <= AB_N, (A_END, B_END)

        A_TOK = ["A:xfm", "A:qfm", "A:kfm", "A:vsb", "A:gsb", "A:otil", "A:ofm", "A:aofm"]
        B_TOK = ["B:x1fm", "B:qhfm", "B:psb", "B:qkhat", "B:h2fm", "B:actfm",
                 "B:prfm", "B:oall", "B:rbc"]
        RO_A = ["b:rr0", "b:rr512", "b:rr1024", "b:rr1536"]
        RO_B = ["b:ro1", "b:ro2"]
        ALIAS_TOK = RO_A + RO_B + ["b:osb", "b:x1rl"]
        ALL_TOK = None

        W_N = 14336
        wbuf = sb("wbuf", [P, W_N], BF16)
        wqk_r = [wbuf[:, i * 1024:(i + 1) * 1024] for i in range(2)]
        wvg_r = [wbuf[:, 2048 + i * 4096:2048 + (i + 1) * 4096] for i in range(2)]
        wo_r = [wbuf[:, 10240 + i * 2048:10240 + (i + 1) * 2048] for i in range(2)]
        wqkv_res = wbuf[:, 0:8 * 1536]
        wfo_r = [wbuf[:, 12288 + i * 512:12288 + (i + 1) * 512] for i in range(2)]
        w1a_r = [wbuf[:, i * 1024:(i + 1) * 1024] for i in range(2)]
        w1g_r = [wbuf[:, 2048 + i * 1024:2048 + (i + 1) * 1024] for i in range(2)]
        w2_r = [wbuf[:, 4096 + i * 2816:4096 + (i + 1) * 2816] for i in range(2)]
        WA_TOK = ["W:qk0", "W:qk1", "W:vg0", "W:vg1", "W:wo0", "W:wo1"]
        WB1_TOK = ["W:qkv", "W:fo0", "W:fo1"]
        WB2_TOK = ["W:w1a0", "W:w1a1", "W:w1g0", "W:w1g1", "W:w20", "W:w21"]
        ALL_ARENA = (A_TOK + B_TOK + WA_TOK + WB1_TOK + WB2_TOK + ALIAS_TOK)

        Sbf = sb("Sbf", [P, 2 * RH * 2 * 512], BF16)
        kxfm = sb("kxfm", [P, 2 * 2 * FH * 512], BF16)
        vfsb = sb("vfsb", [P, 2 * 2 * 4 * FH * 66], BF16)
        xbf = sb("xbf", [P, E], BF16)
        sqs = sb("sqs", [P, E], BF16)
        atil = sb("atil", [P, P], BF16)
        ktil = sb("ktil", [P, 256], BF16)
        qkc1 = sb("qkc1", [P, 128], BF16)
        qkc2 = sb("qkc2", [P, 128], BF16)
        m8n = sb("m8n", [P, FH], BF16)
        lbbc = sb("lbbc", [P, 8], BF16)
        onesr = sb("onesr", [P, P], BF16)
        ones32 = sb("ones32", [P, P])
        epscol = sb("epscol", [P, 2])
        ident = sb("ident", [P, P], BF16)

        rotr = sb("rotr", [P, 2048])
        osb = rotr
        rcos = sb("rcos", [P, 4 * 512], BF16)
        qkct = sb("qkct", [P, 128])
        stats = sb("stats", [P, 16])
        statsb = sb("statsb", [P, 16])
        abcol = sb("abcol", [P, 16])
        stat = sb("stat", [P, 16])
        ssq = sb("ssq", [P, 48])
        rst = sb("rst", [P, 64])
        diag = sb("diag", [P, FH])
        m8 = sb("m8", [P, FH])
        rsum = sb("rsum", [P, 512])
        b2c = sb("b2c", [P, 8])
        carry = sb("carry", [P, 48])
        carrys = sb("carrys", [P, 8])
        gnrep = sb("gnrep", [P, DV], BF16)
        qsrep = sb("qsrep", [P, 2 * DH])
        lggc = sb("lggc", [P, 8])
        bias1a = sb("bias1a", [P, 2 * MSTR])
        bias1g = sb("bias1g", [P, 2 * MSTR])
        bias2 = sb("bias2", [P, 16])
        mtil = sb("mtil", [P, RH * P], BF16)
        gcolt = sb("gcolt", [P, RH])
        gcolks = sb("gcolks", [P, RH])
        tric = sb("tric", [P, P], BF16)
        trif = sb("trif", [P, P], BF16)
        shdn = sb("shdn", [P, P]); shdnb = sb("shdnb", [P, P])
        shup = sb("shup", [P, P]); shupb = sb("shupb", [P, P])

        pb = [pst(f"pb{i}", [P, 512]) for i in range(6)]
        pT1 = pst("pT1", [P, 1024], BF16)
        pT2 = pst("pT2", [P, 1024], BF16)

        sems = {e: st.enter_context(nc.semaphore("s_" + e)) for e in ENGINES}
        block = st.enter_context(nc.Block())

        def dma(eng, out, in_, reads, writes, tag=""):
            return rec.add(eng, lambda q, out=out, in_=in_: q.dma_start(out=out, in_=in_),
                           reads=reads, writes=writes, tag=tag)

        op = rec.add

        wq_rr = [0]

        def wload(slot, hsl, ssl, tok, stok, p, kc, tag, DBG_HOST_W=DBG_HOST_W):
            sview = slot.rearrange("p (kc m) -> p kc m", kc=kc)
            if p == 0:
                dma("gps", sview, hsl.rearrange("(kc p) m -> p kc m", kc=kc),
                    [tok], [tok], tag)
                dma("sync", ssl.rearrange("(kc p) m -> p kc m", kc=kc), sview,
                    [tok], [stok], tag + "sc")
            elif DBG_HOST_W:
                dma("gps", sview, hsl.rearrange("(kc p) m -> p kc m", kc=kc),
                    [tok], [tok], tag)
            else:
                wq_rr[0] ^= 1
                dma("gps" if wq_rr[0] else "sync", sview,
                    ssl.rearrange("(kc p) m -> p kc m", kc=kc),
                    [stok, tok], [tok], tag)

        def dump(name, ap, rtok):
            if name in dump_h:
                dma("gps", dump_h[name], ap, rtok, [f"o:d{name}"], "dump")
                dump_done.append(f"o:d{name}")
        dump_done = []

        dma("gps", mtil[:], dcn["ret_mtil"], [], ["c:mtil"])
        dma("sync", gcolt[:], dcn["ret_gcol_t"], [], ["c:gct"])
        dma("sync", gcolks[:], dcn["ret_gcol_ks"], [], ["c:gcks"])
        dma("gps", tric[:], dcn["tri_causal"], [], ["c:tric"])
        dma("gps", trif[:], dcn["tri_far"], [], ["c:trif"])
        dma("sync", shdn[:], dcn["shift_dn"], [], ["c:shdn"])
        dma("sync", shdnb[:], dcn["shift_dnb"], [], ["c:shdnb"])
        dma("sync", shup[:], dcn["shift_up"], [], ["c:shup"])
        dma("sync", shupb[:], dcn["shift_upb"], [], ["c:shupb"])
        dma("gps", gnrep[:], gn_h.partition_broadcast(P).squeeze(1), [], ["c:gnrep"])
        dma("gps", ident[:], dcn["ident"], [], ["c:ident"])
        op("dve", lambda v: v.memset(onesr[:], 1.0), [], ["c:onesr"])
        op("dve", lambda v: v.memset(ones32[:], 1.0), [], ["c:ones32"])
        op("dve", lambda v: v.memset(epscol[:], 1e-5), [], ["c:eps"])
        op("dve", lambda v: v.memset(Sbf[:], 0.0), [], ["st:S0", "st:S1"])
        op("dve", lambda v: v.memset(carry[:], 0.0), [], ["st:carry0", "st:carry1", "st:carryh0", "st:carryh1"])
        op("dve", lambda v: v.memset(carrys[:], 0.0), [], ["st:carrys0", "st:carrys1"])
        op("dve", lambda v: v.memset(kxfm[:], 0.0), [], ["st:kx00", "st:kx01", "st:kx10", "st:kx11"])
        op("dve", lambda v: v.memset(vfsb[:], 0.0), [], ["st:vf00", "st:vf01", "st:vf10", "st:vf11"])

        def vf_ones(v):
            ap = vfsb[:].rearrange("p (w h c) -> p (w h) c", w=16, h=FH)[:, :, 64:65]
            return v.memset(ap, 1.0)
        op("dve", vf_ones, ["st:vf00", "st:vf01", "st:vf10", "st:vf11"],
           ["st:vf00", "st:vf01", "st:vf10", "st:vf11"])

        state = {"Sp": 0, "layer": None}

        def emit_ln(c, tag):
            chunk = strm[:, c * E:(c + 1) * E]
            sc = f"s:strm{c}"
            op("act", lambda s, chunk=chunk: s.activation(
                sqs[:], chunk, AF.Square, accum_out=stat[:, 1:2]),
               [sc], ["b:st1", "b:sqs"], tag + "sq")
            op("dve", lambda v: v.tensor_scalar_mul(stat[:, 2:3], stat[:, 0:1], 1.0 / E),
               ["b:st0"], ["b:st2"], tag + "mu")
            op("dve", lambda v: v.tensor_tensor(stat[:, 3:4], stat[:, 2:3], stat[:, 2:3], ALU.mult),
               ["b:st2"], ["b:st3"], tag + "mu2")
            op("dve", lambda v: v.tensor_scalar_mul(stat[:, 9:10], stat[:, 1:2], 1.0 / E),
               ["b:st1"], ["b:st9"], tag + "msq")
            op("dve", lambda v: v.tensor_tensor(stat[:, 4:5], stat[:, 9:10], stat[:, 3:4],
                                                ALU.subtract),
               ["b:st9", "b:st3"], ["b:st4"], tag + "var")
            op("act", lambda s: s.activation(stat[:, 5:6], stat[:, 4:5], AF.Sqrt,
                                             bias=epscol[:, 0:1]),
               ["b:st4", "c:eps"], ["b:st5"], tag + "sd")
            op("dve", lambda v: v.reciprocal(stat[:, 6:7], stat[:, 5:6]),
               ["b:st5"], ["b:st6"], tag + "rs")
            op("dve", lambda v: v.tensor_tensor(stat[:, 7:8], stat[:, 2:3], stat[:, 6:7], ALU.mult),
               ["b:st2", "b:st6"], ["b:st7"], tag + "nb")
            op("dve", lambda v: v.tensor_scalar_mul(stat[:, 8:9], stat[:, 7:8], -1.0),
               ["b:st7"], ["b:st8"], tag + "ng")
            op("dve", lambda v, chunk=chunk: v.tensor_scalar(
                chunk, chunk, stat[:, 6:7], stat[:, 8:9], ALU.mult, ALU.add),
               [sc, "b:st6", "b:st8"], [sc], tag + "ap")

        def stageA(p):
            BARRIERS and rec.barrier(f"bA{p}_")
            rec.fence([], ALL_ARENA, f"fA{p}")
            t0 = p * 512
            dma("gps", rcos[:].rearrange("p (g c) -> p g c", g=4),
                dcn["ret_rot4"][:, :, t0:t0 + 512], ["c:rcos"], ["c:rcos"], "rcos")
            dma("sync", strm[:].rearrange("p (c e) -> p c e", c=CPP),
                x_h[p * 512:(p + 1) * 512, :].rearrange("(c p) e -> p c e", c=CPP),
                [f"s:strm{c}" for c in range(CPP)],
                [f"s:strm{c}" for c in range(CPP)], "xload")
            for c in range(CPP):
                gc = p * CPP + c
                dma("gps", xbf[:], x_h[gc * P:(gc + 1) * P, :], ["b:xbf"], ["b:xbf"], "xloadb")

                def xtr(t):
                    last = None
                    for z in range(8):
                        last = t.transpose(pT1[:, z * P:(z + 1) * P],
                                           xbf[:, z * P:(z + 1) * P], ident[:])
                    return last
                op("pe", xtr, ["b:xbf", "c:ident", "p:T1"], ["p:T1"], f"xT{c}")

                def xev(v, c=c):
                    dst = xfm.rearrange("p (e c) -> p e c", e=8)[:, :, c * P:(c + 1) * P]
                    src = pT1[:].rearrange("p (e c) -> p e c", e=8)
                    return v.tensor_copy(dst, src)
                op("dve", xev, ["p:T1"], ["A:xfm"], f"xE{c}")
            for qk in range(2):
                wh = wq_h if qk == 0 else wk_h
                dst = qfm if qk == 0 else kfm
                dtok = "A:qfm" if qk == 0 else "A:kfm"
                co = 0 if qk == 0 else 1024
                for dkc in range(8):
                    sl = dkc % 2
                    wt = wqk_r[sl]
                    wload(wt, wh[dkc * E:(dkc + 1) * E, :],
                          wsc["wq" if qk == 0 else "wk"][dkc * E:(dkc + 1) * E, :],
                          f"W:qk{sl}", f"Wsc:qk{qk}{dkc}", p, 8, "wqk")
                    bank = pb[sl] if qk == 0 else pb[2 + sl]
                    bt = f"p:b{sl}" if qk == 0 else f"p:b{2+sl}"

                    def mm(t, wt=wt, bank=bank):
                        last = None
                        for kc in range(8):
                            last = t.matmul(bank[:], wt[:, kc * P:(kc + 1) * P],
                                            xfm.rearrange("p (e c) -> p e c", e=8)[:, kc, :],
                                            start=(kc == 0), stop=(kc == 7))
                        return last
                    op("pe", mm, [f"W:qk{sl}", "A:xfm", bt], [bt], f"qkp{qk}{dkc}")
                    o1 = 0 if dkc % 2 == 0 else 1024

                    def r1(v, bank=bank, o1=o1, co=co):
                        return v.tensor_tensor(rotr[:, o1:o1 + 512], bank[:],
                                               rcos[:, co:co + 512], ALU.mult)

                    def r2(v, bank=bank, o1=o1, co=co):
                        return v.tensor_tensor(rotr[:, o1 + 512:o1 + 1024], bank[:],
                                               rcos[:, co + 512:co + 1024], ALU.mult)
                    op("dve", r1, [bt, "c:rcos"], [f"b:rr{o1}"], f"r1{qk}{dkc}")
                    op("dve", r2, [bt, "c:rcos"], [f"b:rr{o1+512}"], f"r2{qk}{dkc}")
                    if dkc % 2 == 1:
                        h = dkc // 2

                        def cA(v, dst=dst, h=h):
                            return v.tensor_tensor(dst[:, (2 * h) * 512:(2 * h) * 512 + 512],
                                                   rotr[:, 0:512], rotr[:, 1536:2048], ALU.subtract)

                        def cB(v, dst=dst, h=h):
                            return v.tensor_tensor(dst[:, (2 * h + 1) * 512:(2 * h + 1) * 512 + 512],
                                                   rotr[:, 1024:1536], rotr[:, 512:1024], ALU.add)
                        rr = ["b:rr0", "b:rr512", "b:rr1024", "b:rr1536"]
                        op("dve", cA, rr, [dtok], f"cA{qk}{h}")
                        op("dve", cB, rr, [dtok], f"cB{qk}{h}")
            if p == 0:
                dump("qfm", qfm, ["A:qfm"])
                dump("kfm", kfm, ["A:kfm"])
            for half in range(2):
              chs = [2 * half, 2 * half + 1]
              for vg in range(2):
                wh = wv_h if vg == 0 else wg_h
                dtok = "A:vsb" if vg == 0 else "A:gsb"
                for nb in range(4):
                    sl = nb % 2
                    wt = wvg_r[sl]
                    wload(wt, wh[nb * E:(nb + 1) * E, :],
                          wsc["wv" if vg == 0 else "wg"][nb * E:(nb + 1) * E, :],
                          f"W:vg{sl}", f"Wsc:vg{vg}{nb}", p if half == 0 else 1, 8, "wvg")
                    for lc, c in enumerate(chs):
                        bank = pb[4 + lc]
                        bt = f"p:b{4 + lc}"

                        def mm(t, wt=wt, c=c, bank=bank):
                            last = None
                            for kc in range(8):
                                last = t.matmul(
                                    bank[:],
                                    xfm.rearrange("p (e c) -> p e c", e=8)[:, kc, c * P:(c + 1) * P],
                                    wt[:, kc * 512:(kc + 1) * 512],
                                    start=(kc == 0), stop=(kc == 7))
                            return last
                        op("pe", mm, [f"W:vg{sl}", "A:xfm", bt], [bt], f"vg{vg}{nb}{c}")
                        if vg == 0:
                            def ev(v, lc=lc, nb=nb, bank=bank):
                                return v.tensor_copy(
                                    vsb[:, lc * 2048 + nb * 512:lc * 2048 + (nb + 1) * 512], bank[:])
                        else:
                            def ev(v, lc=lc, nb=nb, bank=bank):
                                return v.tensor_tensor(
                                    gsb[:, lc * 2048 + nb * 512:lc * 2048 + (nb + 1) * 512],
                                    bank[:], gnrep[:], ALU.mult)
                        op("dve", ev, [bt, "c:gnrep"], [dtok], f"vge{vg}{nb}{c}")
              if half == 0:
                  rec.fence([], ALIAS_TOK, f"fR{p}")
              for lc, c in enumerate(chs):
                for h in range(RH):
                    Sp = state["Sp"]
                    hb = h % 2
                    ab = 0 if hb == 0 else 4
                    obk = 1 if hb == 0 else 5
                    atl = atil[:] if hb == 0 else xbf[:, 256:384]
                    atok = "b:atil" if hb == 0 else "b:xbf"
                    ktl = ktil[:] if hb == 0 else xbf[:, 0:256]
                    ktok = "b:ktil" if hb == 0 else "b:xbf"

                    def amm(t, h=h, c=c, ab=ab):
                        last = None
                        for z in range(2):
                            last = t.matmul(
                                pb[ab][:, 0:128],
                                kfm[:, (2 * h + z) * 512 + c * P:(2 * h + z) * 512 + (c + 1) * P],
                                qfm[:, (2 * h + z) * 512 + c * P:(2 * h + z) * 512 + (c + 1) * P],
                                start=(z == 0), stop=(z == 1))
                        return last
                    op("pe", amm, ["A:kfm", "A:qfm", f"p:b{ab}"], [f"p:b{ab}"], f"amm{h}{c}")
                    op("dve", lambda v, h=h, ab=ab, atl=atl: v.tensor_tensor(
                        atl, pb[ab][:, 0:128], mtil[:, h * P:(h + 1) * P], ALU.mult),
                       [f"p:b{ab}", "c:mtil"], [atok], f"atf{h}{c}")

                    def ktr(t, h=h, c=c, hb=hb):
                        last = None
                        for z in range(2):
                            last = t.transpose(
                                pT2[:, hb * 256 + z * P:hb * 256 + (z + 1) * P],
                                kfm[:, (2 * h + z) * 512 + c * P:(2 * h + z) * 512 + (c + 1) * P],
                                ident[:])
                        return last
                    op("pe", ktr, ["A:kfm", "c:ident", "p:T2"], ["p:T2"], f"ktr{h}{c}")
                    op("act", lambda s, h=h, hb=hb, ktl=ktl: s.activation(
                        ktl, pT2[:, hb * 256:hb * 256 + 256], AF.Identity,
                        scale=gcolks[:, h:h + 1], bias=0.0),
                       ["p:T2", "c:gcks"], [ktok], f"kev{h}{c}")

                    def omm(t, h=h, c=c, lc=lc, Sp=Sp, obk=obk, atl=atl):
                        t.matmul(pb[obk][:], atl,
                                 vsb[:, lc * 2048 + h * 512:lc * 2048 + (h + 1) * 512],
                                 start=True, stop=False)
                        last = None
                        for z in range(2):
                            last = t.matmul(
                                pb[obk][:],
                                qfm[:, (2 * h + z) * 512 + c * P:(2 * h + z) * 512 + (c + 1) * P],
                                Sbf[:, (Sp * RH * 2 + h * 2 + z) * 512:(Sp * RH * 2 + h * 2 + z + 1) * 512],
                                start=False, stop=(z == 1))
                        return last
                    op("pe", omm, [atok, "A:vsb", "A:qfm", f"st:S{Sp}", f"p:b{obk}"],
                       [f"p:b{obk}"], f"omm{h}{c}")
                    op("act", lambda s, h=h, obk=obk: s.activation(
                        osb[:, h * 512:(h + 1) * 512], pb[obk][:], AF.Identity,
                        scale=gcolt[:, h:h + 1], bias=0.0),
                       [f"p:b{obk}", "c:gct"], ["b:osb"], f"oev{h}{c}")

                    def dsm(t, h=h, lc=lc, ktl=ktl):
                        last = None
                        for z in range(2):
                            last = t.matmul(pb[2 + z][:], ktl[:, z * P:(z + 1) * P],
                                            vsb[:, lc * 2048 + h * 512:lc * 2048 + (h + 1) * 512],
                                            start=True, stop=True)
                        return last
                    op("pe", dsm, [ktok, "A:vsb", "p:b2", "p:b3"], ["p:b2", "p:b3"],
                       f"dsm{h}{c}")
                    for z in range(2):
                        def su(v, h=h, z=z, Sp=Sp):
                            src = Sbf[:, (Sp * RH * 2 + h * 2 + z) * 512:(Sp * RH * 2 + h * 2 + z + 1) * 512]
                            dst = Sbf[:, ((1 - Sp) * RH * 2 + h * 2 + z) * 512:((1 - Sp) * RH * 2 + h * 2 + z + 1) * 512]
                            return v.scalar_tensor_tensor(dst, in0=src, scalar=GAMMAS[h] ** P,
                                                          in1=pb[2 + z][:], op0=ALU.mult, op1=ALU.add)
                        op("dve", su, [f"st:S{Sp}", f"p:b{2+z}"], [f"st:S{1-Sp}"], f"su{h}{c}{z}")
                state["Sp"] = 1 - state["Sp"]
                for h in range(RH):
                    op("act", lambda s, h=h: s.activation(
                        sqs[:, 0:512], osb[:, h * 512:(h + 1) * 512], AF.Square,
                        accum_out=ssq[:, 32 + h:33 + h]),
                       ["b:osb", "b:sqs"], ["b:ssq", "b:sqs"], f"rsq{h}{c}")
                op("dve", lambda v: v.tensor_scalar_mul(ssq[:, 40:40 + RH], ssq[:, 32:32 + RH],
                                                        1.0 / DV),
                   ["b:ssq"], ["b:ssq2"], f"rmsc{c}")
                op("act", lambda s: s.activation(rst[:, 48:48 + RH], ssq[:, 40:40 + RH], AF.Sqrt,
                                                 bias=epscol[:, 0:1]),
                   ["b:ssq2", "c:eps"], ["b:rst"], f"rms{c}")
                op("dve", lambda v: v.reciprocal(rst[:, 52:52 + RH], rst[:, 48:48 + RH]),
                   ["b:rst"], ["b:rst2"], f"rrc{c}")
                op("dve", lambda v, lc=lc: v.tensor_tensor(
                    otil[:], osb[:], gsb[:, lc * 2048:(lc + 1) * 2048], ALU.mult),
                   ["b:osb", "A:gsb"], ["A:otil"], f"og{c}")

                def ot2(v):
                    src = otil.rearrange("p (h d) -> p h d", h=RH)
                    rsv = rst[:, 52:52 + RH].unsqueeze(2).broadcast_to([P, RH, DV])
                    return v.tensor_tensor(src, src, rsv, ALU.mult)
                op("dve", ot2, ["A:otil", "b:rst2"], ["A:otil"], f"or{c}")
                for hf in range(2):
                    def otr(t, hf=hf):
                        last = None
                        for z in range(8):
                            ic = hf * 8 + z
                            last = t.transpose(pT1[:, z * P:(z + 1) * P],
                                               otil[:, ic * P:(ic + 1) * P], ident[:])
                        return last
                    op("pe", otr, ["A:otil", "c:ident", "p:T1"], ["p:T1"], f"oT{c}{hf}")

                    def oev2(v, lc=lc, hf=hf):
                        dst = ofm.rearrange("p (i c) -> p i c", i=16)[:, hf * 8:(hf + 1) * 8,
                                                                     lc * P:(lc + 1) * P]
                        src = pT1[:].rearrange("p (i c) -> p i c", i=8)
                        return v.tensor_copy(dst, src)
                    op("dve", oev2, ["p:T1"], ["A:ofm"], f"oE{c}{hf}")
              if p == 0 and half == 0:
                  dump("vsb", vsb, ["A:vsb"])
                  dump("gsb", gsb, ["A:gsb"])
                  dump("osb", osb[:], ["b:osb"])
                  dump("otil", otil, ["A:otil"])
                  dump("ofm", ofm, ["A:ofm"])
              BARRIERS and rec.barrier(f"bAWo{p}{half}_")
              for ec in range(8):
                sl = ec % 2
                wt = wo_r[sl]
                wload(wt, wo_h[ec * 2 * E:(ec + 1) * 2 * E, :],
                      wsc["wo"][ec * 2 * E:(ec + 1) * 2 * E, :],
                      f"W:wo{sl}", f"Wsc:wo{ec}", p if half == 0 else 1, 16, "wwo")
                bank = pb[ec % 4]
                bt = f"p:b{ec % 4}"

                def mm(t, wt=wt, bank=bank):
                    last = None
                    for kc in range(16):
                        last = t.matmul(bank[:, 0:256], wt[:, kc * P:(kc + 1) * P],
                                        ofm.rearrange("p (i c) -> p i c", i=16)[:, kc, :],
                                        start=(kc == 0), stop=(kc == 15))
                    return last
                op("pe", mm, [f"W:wo{sl}", "A:ofm", bt], [bt], f"wo{half}{ec}")
                op("dve", lambda v, ec=ec, bank=bank, half=half: v.tensor_copy(
                    aofm[:, ec * 512 + half * 256:ec * 512 + half * 256 + 256], bank[:, 0:256]),
                   [bt], ["A:aofm"], f"woe{half}{ec}")
            if p == 0:
                dump("aofm", aofm, ["A:aofm"])
            for c in range(CPP):
                def atr(t, c=c):
                    last = None
                    for ec in range(8):
                        last = t.transpose(pT2[:, ec * P:(ec + 1) * P],
                                           aofm[:, ec * 512 + c * P:ec * 512 + (c + 1) * P],
                                           ident[:])
                    return last
                op("pe", atr, ["A:aofm", "c:ident", "p:T2"], ["p:T2"], f"aT{c}")
                op("dve", lambda v, c=c: v.scalar_tensor_tensor(
                    strm[:, c * E:(c + 1) * E], in0=pT2[:], scalar=1.0,
                    in1=strm[:, c * E:(c + 1) * E], op0=ALU.mult, op1=ALU.add,
                    accum_out=stat[:, 0:1]),
                   ["p:T2", f"s:strm{c}"], [f"s:strm{c}", "b:st0"], f"res{c}")
                if not DBG_SKIP_LN1:
                    emit_ln(c, f"ln1_{p}_{c}")
                gc = p * CPP + c
                dma("sync", x1sp_h[gc * P:(gc + 1) * P, :], strm[:, c * E:(c + 1) * E],
                    [f"s:strm{c}"], [f"sp:{gc}"], "x1sp")

        def stageB_attn(p, l):
            BARRIERS and rec.barrier(f"bB{p}{l}_")
            rec.fence([], ALL_ARENA, f"fB{p}{l}")
            par = p % 2
            dma("sync", qsrep[:, 0:DH], fqs_h[l:l + 1, :].partition_broadcast(P).squeeze(1),
                ["c:qsr"], ["c:qsr"], "qs")
            dma("sync", qsrep[:, DH:2 * DH], fks_h[l:l + 1, :].partition_broadcast(P).squeeze(1),
                ["c:qsr"], ["c:qsr"], "ks")
            dma("sync", lggc[:], flng_h[l * P:(l + 1) * P, :], ["c:lgg"], ["c:lgg"], "lgg")
            dma("gps", lbbc[:], flnb_h[l * P:(l + 1) * P, :], ["c:lbb"], ["c:lbb"], "lbb")
            dma("sync", bias2[:, l * 8:(l + 1) * 8], fb2_h[l * P:(l + 1) * P, :],
                [f"c:b2_{l}"], [f"c:b2_{l}"], "b2")
            op("dve", lambda v: v.reduce_max(b2c[:, 0:1], qsrep[:, 0:DH], axis=AX.X,
                                             apply_absolute_value=True),
               ["c:qsr"], ["b:b2a"], "mxq")
            op("dve", lambda v: v.reduce_max(b2c[:, 1:2], qsrep[:, DH:2 * DH], axis=AX.X,
                                             apply_absolute_value=True),
               ["c:qsr"], ["b:b2b"], "mxk")
            op("dve", lambda v: v.tensor_tensor(b2c[:, 2:3], b2c[:, 0:1], b2c[:, 1:2], ALU.mult),
               ["b:b2a", "b:b2b"], ["b:b2c"], "mxm")
            op("dve", lambda v: v.tensor_scalar(b2c[:, 3:4], b2c[:, 2:3], QKS, -20.0,
                                                ALU.mult, ALU.add),
               ["b:b2c"], ["b:b2f"], "mxf")
            for kc in range(8):
                wload(wqkv_res[:, kc * 1536:(kc + 1) * 1536],
                      fwqkv_h[l * E + kc * P:l * E + (kc + 1) * P, :],
                      wsc["wqkv"][l * E + kc * P:l * E + (kc + 1) * P, :],
                      "W:qkv", f"Wsc:qkv{l}{kc}", p, 1, "wqkv")
            for c in range(CPP):
                op("dve", lambda v, c=c: v.tensor_copy(xbf[:], strm[:, c * E:(c + 1) * E]),
                   [f"s:strm{c}", "b:xbf"], ["b:xbf"], f"x1b{c}")

                def xtr(t):
                    last = None
                    for z in range(8):
                        last = t.transpose(pT1[:, z * P:(z + 1) * P],
                                           xbf[:, z * P:(z + 1) * P], ident[:])
                    return last
                op("pe", xtr, ["b:xbf", "c:ident", "p:T1"], ["p:T1"], f"x1T{c}")

                def xev(v, c=c):
                    dst = x1fm.rearrange("p (e c) -> p e c", e=8)[:, :, 4 + c * P:4 + (c + 1) * P]
                    src = pT1[:].rearrange("p (e c) -> p e c", e=8)
                    return v.tensor_copy(dst, src)
                op("dve", xev, ["p:T1"], ["B:x1fm"], f"x1E{c}")
            def brst(v, l=l):
                dst = x1fm.rearrange("p (e c) -> p e c", e=8)[:, :, 3:4]
                return v.tensor_copy(dst, carry[:, l * 8:l * 8 + 8].unsqueeze(2))
            op("dve", brst, [f"st:carry{l}", "B:x1fm"], ["B:x1fm"], "brst")

            def bsav(v, l=l):
                src = x1fm.rearrange("p (e c) -> p e c", e=8)[:, :, 515:516]
                return v.tensor_copy(carry[:, l * 8:l * 8 + 8].unsqueeze(2), src)
            op("dve", bsav, ["B:x1fm"], [f"st:carry{l}"], "bsav")

            for c in range(CPP):
                gc = p * CPP + c
                dma("sync", qkct[:], dcn["ffn_ct"][gc * P:(gc + 1) * P, :], ["c:qkct"],
                    ["c:qkct"], "fct")
                def bq1(v):
                    src = qkct[:, 0:32].unsqueeze(1).broadcast_to([P, 2, 32])
                    qs = qsrep[:, 0:DH].rearrange("p (two j) -> p two j", two=2)
                    return v.tensor_tensor(qkc1[:, 0:DH].rearrange("p (two j) -> p two j", two=2),
                                           src, qs, ALU.mult)
                def bk1(v):
                    src = qkct[:, 64:96].unsqueeze(1).broadcast_to([P, 2, 32])
                    ks = qsrep[:, DH:2 * DH].rearrange("p (two j) -> p two j", two=2)
                    return v.tensor_tensor(qkc1[:, DH:2 * DH].rearrange("p (two j) -> p two j", two=2),
                                           src, ks, ALU.mult)
                def bq2(v):
                    src = qkct[:, 32:64].unsqueeze(1).broadcast_to([P, 2, 32])
                    qs = qsrep[:, 0:DH].rearrange("p (two j) -> p two j", two=2)
                    return v.tensor_tensor(qkc2[:, 0:DH].rearrange("p (two j) -> p two j", two=2),
                                           src, qs, ALU.mult)
                def bk2(v):
                    src = qkct[:, 96:128].unsqueeze(1).broadcast_to([P, 2, 32])
                    ks = qsrep[:, DH:2 * DH].rearrange("p (two j) -> p two j", two=2)
                    return v.tensor_tensor(qkc2[:, DH:2 * DH].rearrange("p (two j) -> p two j", two=2),
                                           src, ks, ALU.mult)
                op("dve", bq1, ["c:qkct", "c:qsr"], ["c:qkc1"], "bq1")
                op("dve", bk1, ["c:qkct", "c:qsr"], ["c:qkc1"], "bk1")
                op("dve", bq2, ["c:qkct", "c:qsr"], ["c:qkc2"], "bq2")
                op("dve", bk2, ["c:qkct", "c:qsr"], ["c:qkc2"], "bk2")
                cb = 3 * (c % 2)
                for nb in range(3):
                    def mm(t, c=c, nb=nb, cb=cb):
                        last = None
                        for kc in range(8):
                            colofs = (4 + c * P) if kc < 4 else (3 + c * P)
                            last = t.matmul(
                                pb[cb + nb][:],
                                x1fm.rearrange("p (e c) -> p e c", e=8)[:, kc, colofs:colofs + P],
                                wqkv_res[:, kc * 1536 + nb * 512:kc * 1536 + (nb + 1) * 512],
                                start=(kc == 0), stop=(kc == 7))
                        return last
                    op("pe", mm, ["B:x1fm", "W:qkv", f"p:b{cb+nb}"], [f"p:b{cb+nb}"], f"qkv{c}{nb}")
                def vev(v, par=par, l=l, c=c, cb=cb):
                    dst = vfsb[:].rearrange("p (w ch h cc) -> p w ch h cc", w=4, ch=4,
                                            h=FH)[:, l * 2 + par, c, :, 0:64]
                    src = pb[cb + 2][:].rearrange("p (h d) -> p h d", h=FH)
                    return v.tensor_copy(dst, src)
                op("dve", vev, [f"p:b{cb+2}"], [f"st:vf{l}{par}"], f"vev{c}")
                qks = sqs if c % 2 == 0 else xbf
                sqscr = xbf if c % 2 == 0 else sqs
                qtok = "b:sqs" if c % 2 == 0 else "b:xbf"
                stok2 = "b:xbf" if c % 2 == 0 else "b:sqs"
                so = 16 * (c % 2)
                op("dve", lambda v, cb=cb, qks=qks: v.tensor_copy(qks[:, 0:512], pb[cb][:]),
                   [f"p:b{cb}", qtok], [qtok], f"qe{c}")
                op("dve", lambda v, cb=cb, qks=qks: v.tensor_copy(qks[:, 512:1024], pb[cb + 1][:]),
                   [f"p:b{cb+1}", qtok], [qtok], f"ke{c}")
                op("act", lambda s, qks=qks, sqscr=sqscr: s.activation(sqscr[:], qks[:], AF.Square),
                   [qtok, stok2], [stok2], f"sq{c}")
                op("dve", lambda v, sqscr=sqscr, so=so: v.reduce_sum(
                    ssq[:, so:so + 16], sqscr[:].rearrange("p (g d) -> p g d", g=16), axis=AX.X),
                   [stok2], [f"b:ssq{c%2}"], f"ss{c}")
                op("act", lambda s, so=so: s.activation(rst[:, so:so + 16], ssq[:, so:so + 16],
                                                        AF.Sqrt),
                   [f"b:ssq{c%2}"], [f"b:rst{c%2}"], f"sr{c}")
                op("dve", lambda v, so=so: v.tensor_scalar_max(rst[:, 32 + so:so + 48],
                                                               rst[:, so:so + 16], 1e-12),
                   [f"b:rst{c%2}"], [f"b:rstm{c%2}"], f"sm{c}")
                op("dve", lambda v, so=so: v.reciprocal(rst[:, so:so + 16], rst[:, 32 + so:so + 48]),
                   [f"b:rstm{c%2}"], [f"b:rst{c%2}"], f"si{c}")
                def r1(v, qks=qks):
                    i1 = qkc1[:].rearrange("p (qk d) -> p qk d", qk=2)\
                        .unsqueeze(2).broadcast_to([P, 2, FH, DH])
                    return v.tensor_tensor(
                        rotr[:, 0:1024].rearrange("p (qk h d) -> p qk h d", qk=2, h=FH),
                        qks[:].rearrange("p (qk h d) -> p qk h d", qk=2, h=FH), i1, ALU.mult)
                def r2(v, qks=qks):
                    i1 = qkc2[:].rearrange("p (qk d) -> p qk d", qk=2)\
                        .unsqueeze(2).broadcast_to([P, 2, FH, DH])
                    return v.tensor_tensor(
                        rotr[:, 1024:2048].rearrange("p (qk h d) -> p qk h d", qk=2, h=FH),
                        qks[:].rearrange("p (qk h d) -> p qk h d", qk=2, h=FH), i1, ALU.mult)
                op("dve", r1, [qtok, "c:qkc1"], ["b:ro1"], f"r1{c}")
                op("dve", r2, [qtok, "c:qkc2"], ["b:ro2"], f"r2{c}")
                qkh = qkhat.rearrange("p (h w) -> p h w", h=FH)
                r1v = rotr[:, 0:1024].rearrange("p (qk h two j) -> p qk h two j",
                                                qk=2, h=FH, two=2)
                r2v = rotr[:, 1024:2048].rearrange("p (qk h two j) -> p qk h two j",
                                                   qk=2, h=FH, two=2)
                op("dve", lambda v: v.tensor_tensor(
                    qkh[:, :, 0:32], r1v[:, 0, :, 0], r2v[:, 0, :, 1], ALU.subtract),
                   ["b:ro1", "b:ro2"], ["B:qkhat"], f"hA{c}")
                op("dve", lambda v: v.tensor_tensor(
                    qkh[:, :, 32:64], r1v[:, 0, :, 1], r2v[:, 0, :, 0], ALU.add),
                   ["b:ro1", "b:ro2"], ["B:qkhat"], f"hB{c}")
                op("dve", lambda v: v.tensor_tensor(
                    qkh[:, :, 65:97], r1v[:, 1, :, 0], r2v[:, 1, :, 1], ALU.subtract),
                   ["b:ro1", "b:ro2"], ["B:qkhat"], f"hC{c}")
                op("dve", lambda v: v.tensor_tensor(
                    qkh[:, :, 97:129], r1v[:, 1, :, 1], r2v[:, 1, :, 0], ALU.add),
                   ["b:ro1", "b:ro2"], ["B:qkhat"], f"hD{c}")
                op("dve", lambda v, so=so: v.tensor_tensor(
                    qkh[:, :, 0:64], qkh[:, :, 0:64],
                    rst[:, so:so + 8].unsqueeze(2).broadcast_to([P, 8, 64]), ALU.mult),
                   ["B:qkhat", f"b:rst{c%2}"], ["B:qkhat"], f"hRq{c}")
                op("dve", lambda v, so=so: v.tensor_tensor(
                    qkh[:, :, 65:129], qkh[:, :, 65:129],
                    rst[:, so + 8:so + 16].unsqueeze(2).broadcast_to([P, 8, 64]), ALU.mult),
                   ["B:qkhat", f"b:rst{c%2}"], ["B:qkhat"], f"hRk{c}")
                op("dve", lambda v, sqscr=sqscr: v.tensor_tensor(
                    sqscr[:, 0:512].rearrange("p (h d) -> p h d", h=FH),
                    qkh[:, :, 0:64], qkh[:, :, 65:129], ALU.mult),
                   ["B:qkhat", stok2], [stok2], f"dg{c}")
                op("dve", lambda v, sqscr=sqscr: v.reduce_sum(
                    diag[:], sqscr[:, 0:512].rearrange("p (h d) -> p h d", h=FH), axis=AX.X),
                   [stok2], ["b:diag"], f"dr{c}")
                op("dve", lambda v: v.tensor_scalar(m8[:], diag[:], b2c[:, 3:4], None,
                                                    ALU.max),
                   ["b:diag", "b:b2f"], ["b:m8"], f"dm{c}")
                op("dve", lambda v: v.tensor_scalar_mul(m8n[:], m8[:], -1.0),
                   ["b:m8"], ["b:m8n"], f"dn{c}")
                op("dve", lambda v: v.tensor_copy(qkh[:, :, 64:65], m8n[:].unsqueeze(2)),
                   ["b:m8n"], ["B:qkhat"], f"dc{c}")
                op("dve", lambda v: v.memset(qkh[:, :, 129:130], 1.0),
                   ["B:qkhat"], ["B:qkhat"], f"do{c}")
                def qtr(t):
                    last = None
                    for h in range(FH):
                        last = t.transpose(pT1[0:65, h * P:(h + 1) * P],
                                           qkhat[:, h * 130:h * 130 + 65], ident[:])
                    return last
                op("pe", qtr, ["B:qkhat", "c:ident", "p:T1"], ["p:T1"], f"qT{c}")

                def ktr2(t):
                    last = None
                    for h in range(FH):
                        last = t.transpose(pT2[0:65, h * P:(h + 1) * P],
                                           qkhat[:, h * 130 + 65:h * 130 + 130], ident[:])
                    return last
                op("pe", ktr2, ["B:qkhat", "c:ident", "p:T2"], ["p:T2"], f"kT{c}")

                def qev(v, c=c):
                    dst = qhfm.rearrange("p (h c) -> p h c", h=FH)[0:65, :, c * P:(c + 1) * P]
                    src = pT1[:].rearrange("p (h c) -> p h c", h=FH)[0:65, :, 0:P]
                    return v.tensor_copy(dst, src)
                op("dve", qev, ["p:T1"], ["B:qhfm"], f"qE{c}")

                def kev2(v, c=c, par=par, l=l):
                    dst = kxfm.rearrange("p (w h c) -> p w h c", w=4, h=FH)[0:65, l * 2 + par, :,
                                                                            c * P:(c + 1) * P]
                    src = pT2[:].rearrange("p (h c) -> p h c", h=FH)[0:65, :, 0:P]
                    return v.tensor_copy(dst, src)
                op("dve", kev2, ["p:T2"], [f"st:kx{l}{par}"], f"kE{c}")
            if p == 0 and l == 0:
                dump("x1fm", x1fm, ["B:x1fm"])
                dump("qkhat", qkhat, ["B:qkhat"])
                dump("qhfm", qhfm, ["B:qhfm"])
                dump("kxfm", kxfm[:], [f"st:kx{l}{par}"])
                dump("vfsb", vfsb[:], [f"st:vf{l}{par}"])
            BARRIERS and rec.barrier(f"bAh{p}{l}_")
            w0 = (p == 0)
            for h in range(FH):
                jlist = list(range(4, 8)) if w0 else list(range(8))
                for j in jlist:
                    tc0 = max(0, j - 4)
                    tc1 = min(3, j)
                    span0, ncol = tc0 * P, (tc1 - tc0 + 1) * P
                    jp = par if j >= 4 else 1 - par
                    bk = pb[j % 6]
                    bt = f"p:b{j % 6}"

                    def smm(t, h=h, j=j, jp=jp, span0=span0, ncol=ncol, bk=bk, l=l):
                        kview = kxfm.rearrange("p (w h c) -> p w h c", w=4, h=FH)
                        return t.matmul(bk[:, span0:span0 + ncol],
                                        kview[0:65, l * 2 + jp, h, (j % 4) * P:(j % 4 + 1) * P],
                                        qhfm.rearrange("p (h c) -> p h c", h=FH)[0:65, h,
                                                                                 span0:span0 + ncol],
                                        start=True, stop=True)
                    op("pe", smm, [f"st:kx{l}{jp}", "B:qhfm", bt], [bt], f"sm{h}{j}")
                    if j <= 3:
                        tcc = j
                        op("dve", lambda v, bk=bk, tcc=tcc: v.tensor_tensor(
                            bk[:, tcc * P:(tcc + 1) * P], bk[:, tcc * P:(tcc + 1) * P],
                            trif[:], ALU.add),
                           [bt, "c:trif"], [bt], f"tf{h}{j}")
                    else:
                        tcc = j - 4
                        op("dve", lambda v, bk=bk, tcc=tcc: v.tensor_tensor(
                            bk[:, tcc * P:(tcc + 1) * P], bk[:, tcc * P:(tcc + 1) * P],
                            tric[:], ALU.add),
                           [bt, "c:tric"], [bt], f"tc{h}{j}")
                    op("act", lambda s, j=j, bk=bk, span0=span0, ncol=ncol: s.activation(
                        psb[:, j * 512 + span0:j * 512 + span0 + ncol],
                        bk[:, span0:span0 + ncol], AF.Exp),
                       [bt, "B:psb"], ["B:psb"], f"ex{h}{j}")
                ob = (h % 2) * 64
                first = True
                for j in jlist:
                    tc0 = max(0, j - 4)
                    tc1 = min(3, j)
                    span0, ncol = tc0 * P, (tc1 - tc0 + 1) * P
                    jp = par if j >= 4 else 1 - par

                    def av(t, h=h, j=j, jp=jp, span0=span0, ncol=ncol, first=first,
                           l=l, ob=ob):
                        vview = vfsb[:].rearrange("p (w ch h cc) -> p w ch h cc", w=4,
                                                  ch=4, h=FH)
                        t.matmul(pb[5][ob:ob + 64, span0:span0 + ncol],
                                 vview[:, l * 2 + jp, j % 4, h, 0:64],
                                 psb[:, j * 512 + span0:j * 512 + span0 + ncol],
                                 start=first, stop=(j == jlist[-1]))
                        return t.matmul(pb[4][0:1, span0:span0 + ncol],
                                        vview[:, l * 2 + jp, j % 4, h, 64:65],
                                        psb[:, j * 512 + span0:j * 512 + span0 + ncol],
                                        start=first, stop=(j == jlist[-1]))
                    op("pe", av, [f"st:vf{l}{jp}", "B:psb", "p:b5", "p:b4"],
                       ["p:b5", "p:b4"], f"av{h}{j}")
                    first = False
                op("dve", lambda v: v.tensor_scalar_max(rsum[0:1, :], pb[4][0:1, :], 1e-12),
                   ["p:b4"], ["b:rsum"], f"sc{h}")
                op("dve", lambda v: v.reciprocal(rsum[0:1, :], rsum[0:1, :]),
                   ["b:rsum"], ["b:rsum"], f"si2{h}")
                op("pe", lambda t: t.matmul(pb[3][:, 0:512],
                                            ones32[0:1, 0:128], rsum[0:1, :],
                                            start=True, stop=True),
                   ["c:ones32", "b:rsum", "p:b3"], ["p:b3"], f"rb{h}")
                op("dve", lambda v: v.tensor_copy(rbc[:, :], pb[3][:, :]),
                   ["p:b3", "B:rbc"], ["B:rbc"], f"re{h}")

                def odiv(v, h=h, ob=ob):
                    dst = oall[ob:ob + 64, (h // 2) * 512:(h // 2 + 1) * 512]
                    return v.tensor_tensor(dst, pb[5][ob:ob + 64, :], rbc[ob:ob + 64, :],
                                           ALU.mult)
                op("dve", odiv, ["p:b5", "B:rbc", "B:oall"], ["B:oall"], f"od{h}")
            if p == 0 and l == 0:
                dump("psb", psb, ["B:psb"])
                dump("oall", oall, ["B:oall"])
            BARRIERS and rec.barrier(f"bWo{p}{l}_")
            for ec in range(8):
                sl = ec % 2
                wt = wfo_r[sl]
                wload(wt, fwo_h[(l * 8 + ec) * FH * DH:(l * 8 + ec + 1) * FH * DH, :],
                      wsc["fwo"][(l * 8 + ec) * FH * DH:(l * 8 + ec + 1) * FH * DH, :],
                      f"W:fo{sl}", f"Wsc:fo{l}{ec}", p, 4, "wfo")
                bank = pb[ec % 2]
                bt = f"p:b{ec % 2}"

                def mm(t, wt=wt, bank=bank):
                    last = None
                    for kc in range(4):
                        last = t.matmul(bank[:], wt[:, kc * P:(kc + 1) * P],
                                        oall[:, kc * 512:(kc + 1) * 512],
                                        start=(kc == 0), stop=(kc == 3))
                    return last
                op("pe", mm, [f"W:fo{sl}", "B:oall", bt], [bt], f"fo{ec}")
                op("dve", lambda v, ec=ec, bank=bank: v.tensor_copy(
                    prfm[:, ec * 512:(ec + 1) * 512], bank[:]),
                   [bt], ["B:prfm"], f"foe{ec}")
            for c in range(CPP):
                def atr(t, c=c):
                    last = None
                    for ec in range(8):
                        last = t.transpose(pT2[:, ec * P:(ec + 1) * P],
                                           prfm[:, ec * 512 + c * P:ec * 512 + (c + 1) * P],
                                           ident[:])
                    return last
                op("pe", atr, ["B:prfm", "c:ident", "p:T2"], ["p:T2"], f"bT{c}")
                op("dve", lambda v, c=c: v.tensor_tensor(
                    strm[:, c * E:(c + 1) * E], pT2[:], strm[:, c * E:(c + 1) * E], ALU.add),
                   ["p:T2", f"s:strm{c}"], [f"s:strm{c}"], f"bre{c}")

        def stageB_mlp(p, l):
            BARRIERS and rec.barrier(f"bM{p}{l}_")
            rec.fence([], WB1_TOK + WB2_TOK + ["B:psb", "B:oall", "B:rbc", "B:qkhat", "B:qhfm"], f"fM{p}{l}")
            for c in range(CPP):
                op("dve", lambda v, c=c: v.reduce_sum(
                    stats[:, c * 4:c * 4 + 2],
                    strm[:, c * E:(c + 1) * E].rearrange("p (g d) -> p g d", g=2), axis=AX.X),
                   [f"s:strm{c}"], [f"b:sst{c}"], f"ls{c}")
                op("act", lambda s, c=c: s.activation(sqs[:], strm[:, c * E:(c + 1) * E],
                                                      AF.Square),
                   [f"s:strm{c}", "b:sqs"], ["b:sqs"], f"lq{c}")
                op("dve", lambda v, c=c: v.reduce_sum(
                    stats[:, c * 4 + 2:c * 4 + 4],
                    sqs[:].rearrange("p (g d) -> p g d", g=2), axis=AX.X),
                   ["b:sqs"], [f"b:sst{c}"], f"lr{c}")
            for c in range(CPP):
                def shf(t, c=c, l=l):
                    t.matmul(pb[2][:, 0:4], shdn[:], stats[:, c * 4:(c + 1) * 4],
                             start=True, stop=False)
                    prev = carrys[:, l * 4:l * 4 + 4] if c == 0 else stats[:, (c - 1) * 4:c * 4]
                    return t.matmul(pb[2][:, 0:4], shdnb[:], prev, start=False, stop=True)
                rdep = [f"b:sst{c}", "c:shdn", "c:shdnb", "p:b2"]
                rdep += [f"st:carrys{l}"] if c == 0 else [f"b:sst{c-1}"]
                op("pe", shf, rdep, ["p:b2"], f"sh{c}")
                op("dve", lambda v, c=c: v.tensor_copy(statsb[:, c * 4:(c + 1) * 4],
                                                       pb[2][:, 0:4]),
                   ["p:b2"], [f"b:ssb{c}"], f"she{c}")
            op("dve", lambda v, l=l: v.tensor_copy(carrys[:, l * 4:l * 4 + 4], stats[:, 12:16]),
               ["b:sst3"], [f"st:carrys{l}"], "scar")
            for c in range(CPP):
                op("dve", lambda v, c=c: v.tensor_tensor(
                    stat[:, 0:2],
                    stats[:, c * 4:c * 4 + 4].rearrange("p (g d) -> p d g", g=2)[:, 0, :],
                    statsb[:, c * 4:c * 4 + 4].rearrange("p (g d) -> p d g", g=2)[:, 1, :],
                    ALU.add),
                   [f"b:sst{c}", f"b:ssb{c}"], ["b:st01"], f"ab0{c}")
                op("dve", lambda v: v.tensor_scalar_mul(stat[:, 2:4], stat[:, 0:2], 1.0 / E),
                   ["b:st01"], ["b:st23"], f"ab1{c}")
                op("dve", lambda v: v.tensor_tensor(stat[:, 4:5], stat[:, 2:3], stat[:, 2:3],
                                                    ALU.mult),
                   ["b:st23"], ["b:st45"], f"ab2{c}")
                op("dve", lambda v: v.tensor_tensor(stat[:, 5:6], stat[:, 3:4], stat[:, 4:5],
                                                    ALU.subtract),
                   ["b:st23", "b:st45"], ["b:st45"], f"ab3{c}")
                op("act", lambda s: s.activation(stat[:, 6:7], stat[:, 5:6], AF.Sqrt,
                                                 bias=epscol[:, 0:1]),
                   ["b:st45", "c:eps"], ["b:st6b"], f"ab4{c}")
                op("dve", lambda v, c=c: v.reciprocal(abcol[:, c * 4:c * 4 + 1], stat[:, 6:7]),
                   ["b:st6b"], [f"b:abc{c}"], f"ab5{c}")
                op("dve", lambda v, c=c: v.tensor_tensor(
                    stat[:, 7:8], stat[:, 2:3], abcol[:, c * 4:c * 4 + 1], ALU.mult),
                   ["b:st23", f"b:abc{c}"], ["b:st7b"], f"ab6{c}")
                op("dve", lambda v, c=c: v.tensor_scalar_mul(
                    abcol[:, c * 4 + 1:c * 4 + 2], stat[:, 7:8], -1.0),
                   ["b:st7b"], [f"b:abc{c}"], f"ab7{c}")
            for c in range(CPP):
                def shu2(t, c=c):
                    last = t.matmul(pb[2][:, 0:2], shup[:], abcol[:, c * 4:c * 4 + 2],
                                    start=True, stop=(c == 3))
                    if c < 3:
                        last = t.matmul(pb[2][:, 0:2], shupb[:],
                                        abcol[:, (c + 1) * 4:(c + 1) * 4 + 2],
                                        start=False, stop=True)
                    return last
                rdep = ["c:shup", "c:shupb", f"b:abc{c}", "p:b2"]
                if c < 3:
                    rdep.append(f"b:abc{c+1}")
                op("pe", shu2, rdep, ["p:b2"], f"su{c}")
                op("dve", lambda v, c=c: v.tensor_copy(abcol[:, c * 4 + 2:c * 4 + 4],
                                                       pb[2][:, 0:2]),
                   ["p:b2"], [f"b:abc{c}"], f"sue{c}")
            op("pe", lambda t: t.matmul(pb[3][:, 0:2], ones32[0:1, 0:128], abcol[0:1, 0:2],
                                        start=True, stop=True),
               ["c:ones32", "b:abc0", "p:b3"], ["p:b3"], "a0b")
            op("dve", lambda v: v.tensor_copy(b2c[:, 4:6], pb[3][:, 0:2]),
               ["p:b3"], ["b:a0"], "a0e")
            op("dve", lambda v, l=l: v.scalar_tensor_tensor(
                stat[:, 8:12].rearrange("p (a b) -> p a b", b=1),
                in0=carry[:, 16 + l * 4:16 + l * 4 + 4].rearrange("p (a b) -> p a b", b=1),
                scalar=b2c[:, 4:5],
                in1=b2c[:, 5:6].unsqueeze(1).broadcast_to([P, 4, 1]),
                op0=ALU.mult, op1=ALU.add),
               [f"st:carryh{l}", "b:a0"], ["b:bcol"], "bc0")
            op("dve", lambda v: v.tensor_tensor(stat[:, 8:12], stat[:, 8:12], lggc[:, 4:8],
                                                ALU.mult),
               ["b:bcol", "c:lgg"], ["b:bcol"], "bc1")
            def bcw(v):
                dst = h2fm.rearrange("p (e c) -> p e c", e=8)[:, 4:8, 4:5]
                return v.tensor_copy(dst, stat[:, 8:12].unsqueeze(2))
            op("dve", bcw, ["b:bcol", "B:h2fm"], ["B:h2fm"], "bc2")
            for c in range(CPP):
                op("dve", lambda v, c=c: v.tensor_scalar(
                    xbf[:, 0:512], strm[:, c * E:c * E + 512],
                    abcol[:, c * 4:c * 4 + 1], abcol[:, c * 4 + 1:c * 4 + 2],
                    ALU.mult, ALU.add),
                   [f"s:strm{c}", f"b:abc{c}", "b:xbf"], ["b:xbf"], f"apA{c}")
                op("dve", lambda v, c=c: v.tensor_scalar(
                    xbf[:, 512:1024], strm[:, c * E + 512:(c + 1) * E],
                    abcol[:, c * 4 + 2:c * 4 + 3], abcol[:, c * 4 + 3:c * 4 + 4],
                    ALU.mult, ALU.add),
                   [f"s:strm{c}", f"b:abc{c}", "b:xbf"], ["b:xbf"], f"apB{c}")

                def htr(t):
                    last = None
                    for z in range(8):
                        last = t.transpose(pT1[:, z * P:(z + 1) * P],
                                           xbf[:, z * P:(z + 1) * P], ident[:])
                    return last
                op("pe", htr, ["b:xbf", "c:ident", "p:T1"], ["p:T1"], f"hT{c}")
                for z in range(8):
                    colofs = (4 + c * P) if z < 4 else (5 + c * P)

                    def hev(v, z=z, colofs=colofs):
                        dst = h2fm[:, z * 520 + colofs:z * 520 + colofs + P]
                        return v.tensor_scalar_mul(dst, pT1[:, z * P:(z + 1) * P],
                                                   lggc[:, z:z + 1])
                    op("dve", hev, ["p:T1", "c:lgg", "B:h2fm"], ["B:h2fm"], f"hE{c}{z}")
            op("dve", lambda v: v.tensor_copy(xbf[:, 0:512], strm[:, 3 * E + 512:4 * E]),
               ["s:strm3", "b:xbf"], ["b:xbf"], "rcp")

            def rtr(t):
                last = None
                for z in range(4):
                    last = t.transpose(pT1[:, z * P:(z + 1) * P],
                                       xbf[:, z * P:(z + 1) * P], ident[:])
                return last
            op("pe", rtr, ["b:xbf", "c:ident", "p:T1"], ["p:T1"], "rT")

            def rcw(v, l=l):
                src = pT1[:].rearrange("p (e c) -> p e c", e=8)[:, 0:4, 127:128]
                return v.tensor_copy(carry[:, 16 + l * 4:16 + l * 4 + 4].unsqueeze(2), src)
            op("dve", rcw, ["p:T1"], [f"st:carryh{l}"], "rcw")
            BARRIERS and rec.barrier(f"bW1{p}{l}_")
            if p == 0:
                dma("sync", bias1a[:, l * MSTR:(l + 1) * MSTR], fb1a_h[l * P:(l + 1) * P, :],
                    [f"c:b1a{l}"], [f"c:b1a{l}"], "b1a")
                dma("sync", bias1g[:, l * MSTR:(l + 1) * MSTR], fb1g_h[l * P:(l + 1) * P, :],
                    [f"c:b1g{l}"], [f"c:b1g{l}"], "b1g")
            for mstr in range(MSTR):
                sl = mstr % 2
                wta, wtg = w1a_r[sl], w1g_r[sl]
                base = (l * MSTR + mstr) * E
                wload(wta, fw1a_h[base:base + E, :], wsc["w1a"][base:base + E, :],
                      f"W:w1a{sl}", f"Wsc:w1a{l}{mstr}", p, 8, "w1a")
                wload(wtg, fw1g_h[base:base + E, :], wsc["w1g"][base:base + E, :],
                      f"W:w1g{sl}", f"Wsc:w1g{l}{mstr}", p, 8, "w1g")
                if p == 0:
                    for ag in range(2):
                        wt = wta if ag == 0 else wtg
                        bia = bias1a if ag == 0 else bias1g
                        btk = f"c:b1a{l}" if ag == 0 else f"c:b1g{l}"

                        def lbm(t, wt=wt):
                            last = None
                            for kc in range(8):
                                last = t.matmul(pb[4][:, 0:1], wt[:, kc * P:(kc + 1) * P],
                                                lbbc[:, kc:kc + 1], start=(kc == 0),
                                                stop=(kc == 7))
                            return last
                        op("pe", lbm, [f"W:w1a{sl}" if ag == 0 else f"W:w1g{sl}",
                                       "c:lbb", "p:b4"], ["p:b4"], f"lb{mstr}{ag}")
                        op("dve", lambda v, bia=bia, mstr=mstr: v.tensor_tensor(
                            bia[:, l * MSTR + mstr:l * MSTR + mstr + 1],
                            pb[4][:, 0:1],
                            bia[:, l * MSTR + mstr:l * MSTR + mstr + 1], ALU.add),
                           ["p:b4", btk], [btk], f"lbe{mstr}{ag}")
                for ag in range(2):
                    wt = wta if ag == 0 else wtg
                    wtk = f"W:w1a{sl}" if ag == 0 else f"W:w1g{sl}"
                    bank = pb[ag]
                    bt = f"p:b{ag}"

                    def mm(t, wt=wt, bank=bank):
                        last = None
                        for kc in range(8):
                            last = t.matmul(bank[:],
                                            wt[:, kc * P:(kc + 1) * P],
                                            h2fm.rearrange("p (e c) -> p e c", e=8)[:, kc, 4:516],
                                            start=(kc == 0), stop=(kc == 7))
                        return last
                    op("pe", mm, [wtk, "B:h2fm", bt], [bt], f"w1{mstr}{ag}")
                op("act", lambda s, mstr=mstr: s.activation(
                    sqs[:, 0:512], pb[0][:], AF.Identity,
                    bias=bias1a[:, l * MSTR + mstr:l * MSTR + mstr + 1]),
                   ["p:b0", f"c:b1a{l}", "b:sqs"], ["b:sqs"], f"ga{mstr}")
                op("act", lambda s, mstr=mstr: s.activation(
                    sqs[:, 512:1024], pb[1][:], AF.Gelu,
                    bias=bias1g[:, l * MSTR + mstr:l * MSTR + mstr + 1]),
                   ["p:b1", f"c:b1g{l}", "b:sqs"], ["b:sqs"], f"gg{mstr}")
                op("dve", lambda v, mstr=mstr: v.tensor_tensor(
                    actfm[:, mstr * 512:(mstr + 1) * 512], sqs[:, 0:512], sqs[:, 512:1024],
                    ALU.mult),
                   ["b:sqs", "B:actfm"], ["B:actfm"], f"gm{mstr}")
            if p == 0 and l == 0:
                dump("h2fm", h2fm, ["B:h2fm"])
                dump("actfm", actfm, ["B:actfm"])
            BARRIERS and rec.barrier(f"bW2{p}{l}_")
            for ec in range(8):
                sl = ec % 2
                wt = w2_r[sl]
                base = (l * 8 + ec) * MSTR * P
                wload(wt, fw2_h[base:base + MSTR * P, :], wsc["w2"][base:base + MSTR * P, :],
                      f"W:w2{sl}", f"Wsc:w2{l}{ec}", p, MSTR, "w2l")
                bank = pb[2 + ec % 2]
                bt = f"p:b{2 + ec % 2}"

                def mm(t, wt=wt, bank=bank):
                    last = None
                    for kc in range(MSTR):
                        last = t.matmul(bank[:], wt[:, kc * P:(kc + 1) * P],
                                        actfm[:, kc * 512:(kc + 1) * 512],
                                        start=(kc == 0), stop=(kc == MSTR - 1))
                    return last
                op("pe", mm, [f"W:w2{sl}", "B:actfm", bt], [bt], f"w2{ec}")
                op("act", lambda s, ec=ec, bank=bank: s.activation(
                    prfm[:, ec * 512:(ec + 1) * 512], bank[:], AF.Identity,
                    bias=bias2[:, l * 8 + ec:l * 8 + ec + 1]),
                   [bt, f"c:b2_{l}", "B:prfm"], ["B:prfm"], f"w2e{ec}")
            for c in range(CPP):
                def atr(t, c=c):
                    last = None
                    for ec in range(8):
                        last = t.transpose(pT2[:, ec * P:(ec + 1) * P],
                                           prfm[:, ec * 512 + c * P:ec * 512 + (c + 1) * P],
                                           ident[:])
                    return last
                op("pe", atr, ["B:prfm", "c:ident", "p:T2"], ["p:T2"], f"wT{c}")
                op("dve", lambda v, c=c: v.tensor_tensor(
                    strm[:, c * E:(c + 1) * E], pT2[:], strm[:, c * E:(c + 1) * E], ALU.add),
                   ["p:T2", f"s:strm{c}"], [f"s:strm{c}"], f"wre{c}")

        def stageF(p):
            BARRIERS and rec.barrier(f"bF{p}_")
            rec.fence([], ALIAS_TOK, f"fF{p}")
            x1rl = rotr
            for c in range(CPP):
                gc = p * CPP + c
                dma("sync", x1rl[:, 0:1024], x1sp_h[gc * P:(gc + 1) * P, :],
                    [f"sp:{gc}", "b:x1rl"], ["b:x1rl"], "x1rl")
                op("dve", lambda v, c=c: v.scalar_tensor_tensor(
                    strm[:, c * E:(c + 1) * E], in0=x1rl[:, 0:1024], scalar=1.0,
                    in1=strm[:, c * E:(c + 1) * E], op0=ALU.mult, op1=ALU.add,
                    accum_out=stat[:, 0:1]),
                   ["b:x1rl", f"s:strm{c}"], [f"s:strm{c}", "b:st0"], f"fr{c}")
                emit_ln(c, f"lnF_{p}_{c}")

        def store_out(p):
            for c in range(CPP):
                gc = p * CPP + c
                dma("sync", out_h[gc * P:(gc + 1) * P, :], strm[:, c * E:(c + 1) * E],
                    [f"s:strm{c}"], [f"o:{gc}"], "ost")

        for p in range(NPASS):
            stageA(p)
            if "B0a" in stages or "B0" in stages:
                stageB_attn(p, 0)
            if "B0" in stages:
                stageB_mlp(p, 0)
            if "B1" in stages:
                stageB_attn(p, 1)
                stageB_mlp(p, 1)
            if "F" in stages:
                stageF(p)
            store_out(p)

        rec.add("sync", lambda q: q.nop(),
                reads=[f"o:{gc}" for gc in range(NPASS * CPP)] + dump_done,
                writes=[], tag="fin")

        engmap = {"pe": block.tensor, "act": block.scalar, "dve": block.vector,
                  "sync": block.sync, "gps": block.gpsimd}
        rec.emit(sems, engmap)

    return nc, cns


def kernel(**inputs):
    nc, cns = build_program()
    in_maps = [make_core_inputs(inputs, core, cns) for core in range(8)]
    res = run_bass_kernel_spmd(nc, in_maps, core_ids=list(range(8)))
    out = np.stack([res.results[i]["out"] for i in range(8)], axis=0)
    return np.ascontiguousarray(out.astype(np.float32))


# version 17
# speedup vs baseline: 1.0195x; 1.0195x over previous
import numpy as np
import ml_dtypes
import concourse.bass as bass
import concourse.mybir as mybir
from concourse.bass_utils import run_bass_kernel_spmd
from contextlib import ExitStack

F32 = mybir.dt.float32
BF16 = mybir.dt.bfloat16
AF = mybir.ActivationFunctionType
ALU = mybir.AluOpType
AX = mybir.AxisListType

P = 128
T = 4096
E = 1024
NPASS = 8
CPP = 4
NEG = -1e9

RH, DK, DV = 4, 256, 512
GAMMAS = [1.0 - 2.0 ** (-5.0 - h) for h in range(RH)]
FH, DH = 8, 64
HID = 2730
MSTR = 22
WIN = 512
QKS = 8.0

ENGINES = ("pe", "act", "dve", "sync", "gps")
DMA_ENG = {"sync", "gps"}


class Rec:
    def __init__(self):
        self.ops = {e: [] for e in ENGINES}
        self.all = []
        self.last_writer = {}
        self.readers = {}
        self.bar_arr = []
        self.bar_wait = []
        self.seg = 0

    def add(self, eng, fn, reads=(), writes=(), tag=""):
        pos = len(self.ops[eng])
        deps = {}
        rawdeps = []
        for b in tuple(reads) + tuple(writes):
            w = self.last_writer.get(b)
            if w is not None:
                deps[w[0]] = max(deps.get(w[0], -1), w[1])
                rawdeps.append(w)
        for b in writes:
            for r in self.readers.get(b, ()):
                deps[r[0]] = max(deps.get(r[0], -1), r[1])
                rawdeps.append(r)
        o = dict(eng=eng, fn=fn, deps=deps, inc=0, waits=[], tag=tag, pos=pos,
                 force_inc=False, is_barw=None, seg=self.seg,
                 dep_ops=[self.ops[e2][p2] for (e2, p2) in set(rawdeps)])
        self.ops[eng].append(o)
        self.all.append(o)
        for b in reads:
            self.readers.setdefault(b, []).append((eng, pos))
        for b in writes:
            self.last_writer[b] = (eng, pos)
            self.readers[b] = []
        return o

    def fence(self, reads, writes, tag="fence"):
        self.add("dve", lambda v: v.engine_nop(), reads=reads, writes=writes, tag=tag)

    def barrier(self, tag="bar"):
        self.seg += 1
        bid = len(self.bar_arr)
        arr = {"__seg__": self.seg}
        for e in ENGINES:
            if e in DMA_ENG:
                continue
            o = self.add(e, lambda q: q.drain(), [], [], tag=f"{tag}{bid}a_{e}")
            o["force_inc"] = True
            arr[e] = o
        self.bar_arr.append(arr)
        for e in ENGINES:
            if e in DMA_ENG:
                continue
            o = self.add(e, lambda q: q.nop(), [], [], tag=f"{tag}{bid}w_{e}")
            o["is_barw"] = bid
            self.bar_wait.append(o)
        self.seg += 1

    COST = {"pe": 400, "act": 600, "dve": 600, "sync": 1500, "gps": 2500}
    BIGPE = ("qkp", "vg", "wo", "w1", "w2", "qkv", "fo")
    TRPE = ("xT", "x1T", "hT", "qT", "kT", "ktr", "oT", "aT", "bT", "wT", "rT")

    def opcost(self, o):
        t = o["tag"]
        if o["eng"] == "pe":
            if t.startswith(self.BIGPE):
                return 2000
            if t.startswith(self.TRPE):
                return 1200
            return 500
        return self.COST[o["eng"]]

    def schedule(self):
        import heapq
        for o in self.all:
            o["nsucc"] = 0
            o["cp"] = self.opcost(o)
        for o in self.all:
            for d in o["dep_ops"]:
                d["nsucc"] += 1
        for o in reversed(self.all):
            for d in o["dep_ops"]:
                c = o["cp"] + self.opcost(d)
                if c > d["cp"]:
                    d["cp"] = c
        neworder = []
        i = 0
        allops = self.all
        n = len(allops)
        for o in allops:
            o["unmet"] = len(o["dep_ops"])
            o["succ"] = []
        for o in allops:
            for d in o["dep_ops"]:
                d["succ"].append(o)
        gidx = {id(o): k for k, o in enumerate(allops)}
        while i < n:
            seg = allops[i]["seg"]
            j = i
            while j < n and allops[j]["seg"] == seg:
                j += 1
            segops = allops[i:j]
            if seg % 2 == 1:
                neworder.extend(segops)
                for o in segops:
                    for s in o["succ"]:
                        s["unmet"] -= 1
            else:
                ready = []
                for o in segops:
                    if o["unmet"] == 0:
                        heapq.heappush(ready, (-o["cp"], gidx[id(o)], id(o), o))
                insg = set(id(o) for o in segops)
                done = 0
                byid = {id(o): o for o in segops}
                while done < len(segops):
                    assert ready, "scheduler stuck"
                    _, _, _, o = heapq.heappop(ready)
                    neworder.append(o)
                    done += 1
                    for s in o["succ"]:
                        s["unmet"] -= 1
                        if s["unmet"] == 0 and id(s) in insg:
                            heapq.heappush(ready, (-s["cp"], gidx[id(s)], id(s), s))
            i = j
        self.all = neworder
        self.ops = {e: [] for e in ENGINES}
        for o in self.all:
            o["pos"] = len(self.ops[o["eng"]])
            self.ops[o["eng"]].append(o)
        for o in self.all:
            deps = {}
            for d in o["dep_ops"]:
                deps[d["eng"]] = max(deps.get(d["eng"], -1), d["pos"])
            o["deps"] = deps
        self.finalize()

    def finalize(self):
        needed = {e: set() for e in ENGINES}
        for op in self.all:
            for (pe_, pp) in op["deps"].items():
                if pe_ == "pe" and op["eng"] == "pe":
                    continue
                needed[pe_].add(pp)
        self.cum = {}
        for e in ENGINES:
            cnt = 0
            step = 16 if e in DMA_ENG else 1
            for i, op in enumerate(self.ops[e]):
                if op["is_barw"] is None and (e in DMA_ENG or i in needed[e] or op["force_inc"]):
                    op["inc"] = step
                    cnt += step
                self.cum[(e, i)] = cnt
        for op in self.all:
            byeng = {}
            for (pe_, pp) in op["deps"].items():
                if pe_ == "pe" and op["eng"] == "pe":
                    continue
                ops_p = self.ops[pe_]
                j = pp
                while j < len(ops_p) and ops_p[j]["inc"] == 0:
                    j += 1
                assert j < len(ops_p), f"no inc on {pe_} after {pp} for {op['tag']}"
                byeng[pe_] = max(byeng.get(pe_, 0), self.cum[(pe_, j)])
            if op["eng"] in DMA_ENG and op["pos"] > 0:
                prev = self.cum[(op["eng"], op["pos"] - 1)]
                byeng[op["eng"]] = max(byeng.get(op["eng"], 0), prev)
            if op["is_barw"] is not None:
                arr = self.bar_arr[op["is_barw"]]
                barseg = arr["__seg__"]
                for e2 in ENGINES:
                    if e2 == op["eng"]:
                        continue
                    if e2 in DMA_ENG:
                        last = -1
                        for k, o2 in enumerate(self.ops[e2]):
                            if o2["seg"] < barseg:
                                last = k
                            else:
                                break
                        if last >= 0:
                            byeng[e2] = max(byeng.get(e2, 0), self.cum[(e2, last)])
                    else:
                        byeng[e2] = max(byeng.get(e2, 0),
                                        self.cum[(e2, arr[e2]["pos"])])
            op["waits"] = sorted(byeng.items())

    def emit(self, sems, engmap):
        self.schedule()
        for e in ENGINES:
            if not self.ops[e]:
                continue

            def body(eng, e=e):
                for op in self.ops[e]:
                    for (pe_, val) in op["waits"]:
                        eng.wait_ge(sems[pe_], val)
                    inst = op["fn"](eng)
                    if op["inc"]:
                        assert inst is not None, f"{e}:{op['tag']} returned None"
                        if e in ("dve", "act", "pe"):
                            eng.drain().then_inc(sems[e], op["inc"])
                        else:
                            inst.then_inc(sems[e], op["inc"])
            engmap[e](body)


def host_constants():
    c = {}
    c["ident"] = np.eye(P, dtype=np.float32)
    mtil = np.zeros((P, RH * P), np.float32)
    gct = np.zeros((P, RH), np.float32)
    gcks = np.zeros((P, RH), np.float32)
    for h, g in enumerate(GAMMAS):
        s = np.arange(P)
        mtil[:, h * P:(h + 1) * P] = np.where(
            np.arange(P)[None, :] >= s[:, None],
            np.power(g, -s.astype(np.float64))[:, None], 0.0)
        gct[:, h] = np.power(g, np.arange(P, dtype=np.float64))
        gcks[:, h] = np.power(g, P - np.arange(P, dtype=np.float64))
    c["ret_mtil"] = mtil
    c["ret_gcol_t"] = gct
    c["ret_gcol_ks"] = gcks
    j = np.arange(P, dtype=np.float64)
    inv = 1.0 / (10000.0 ** (2 * j / DK))
    fr = np.outer(inv, np.arange(T, dtype=np.float64))
    c["ret_rot4"] = np.stack([np.cos(fr) / 16.0, np.sin(fr) / 16.0,
                              np.cos(fr), np.sin(fr)], axis=1).astype(np.float32)
    hj = np.arange(0, DH, 2, dtype=np.float64)
    invf = 1.0 / (10000.0 ** (hj / DH))
    fr2 = np.outer(np.arange(T, dtype=np.float64), invf)
    base = (hj + 0.4 * DH) / (1.4 * DH)
    power = (np.arange(T, dtype=np.float64) - T // 2) / (WIN // 2)
    scale = base[None, :] ** power[:, None]
    c["ffn_ct"] = np.concatenate(
        [np.cos(fr2) * scale * QKS, np.sin(fr2) * scale * QKS,
         np.cos(fr2) / scale, np.sin(fr2) / scale], axis=1).astype(np.float32)
    sl = np.arange(P)[:, None]
    tl = np.arange(P)[None, :]
    c["tri_causal"] = np.where(sl <= tl, 0.0, NEG).astype(np.float32)
    c["tri_far"] = np.where(sl >= tl, 0.0, NEG).astype(np.float32)
    psh = np.zeros((P, P), np.float32)
    psh[np.arange(P - 1), np.arange(1, P)] = 1.0
    c["shift_dn"] = psh
    pshb = np.zeros((P, P), np.float32); pshb[P - 1, 0] = 1.0
    c["shift_dnb"] = pshb
    psu = np.zeros((P, P), np.float32)
    psu[np.arange(1, P), np.arange(P - 1)] = 1.0
    c["shift_up"] = psu
    psub = np.zeros((P, P), np.float32); psub[0, P - 1] = 1.0
    c["shift_upb"] = psub
    return c


def make_core_inputs(inp, core, cns):
    m = {}
    m["x"] = np.asarray(inp["L0_attn_x"][core], np.float32)
    wq = np.asarray(inp["L0_attn_Wq"], np.float32)
    wk = np.asarray(inp["L0_attn_Wk"], np.float32)
    wv = np.asarray(inp["L0_attn_Wv"], np.float32)
    wg = np.asarray(inp["L0_attn_Wg"], np.float32)
    wo = np.asarray(inp["L0_attn_Wo"], np.float32)
    m["wq_s"] = wq.reshape(E, 8, P).transpose(1, 0, 2).reshape(8 * E, P)
    m["wk_s"] = wk.reshape(E, 8, P).transpose(1, 0, 2).reshape(8 * E, P)
    m["wv_s"] = wv.reshape(E, 4, DV).transpose(1, 0, 2).reshape(4 * E, DV)
    m["wg_s"] = wg.reshape(E, 4, DV).transpose(1, 0, 2).reshape(4 * E, DV)
    m["wo_s"] = wo.reshape(2 * E, 8, P).transpose(1, 0, 2).reshape(8 * 2 * E, P)
    m["gn"] = np.asarray(inp["L0_attn_gn_weight"], np.float32).reshape(1, DV)
    m["fwqkv"] = np.asarray(inp["L0_ffn_wqkv"], np.float32).reshape(2 * E, 3 * FH * DH)
    m["fqs"] = np.asarray(inp["L0_ffn_q_scale"], np.float32)
    m["fks"] = np.asarray(inp["L0_ffn_k_scale"], np.float32)
    fwo = np.asarray(inp["L0_ffn_wo"], np.float32)
    m["fwo_s"] = fwo.reshape(2, FH * DH, 8, P).transpose(0, 2, 1, 3).reshape(2 * 8 * FH * DH, P)
    lng = np.asarray(inp["L0_ffn_ln_g"], np.float32)
    lnb = np.asarray(inp["L0_ffn_ln_b"], np.float32)
    m["flng_c"] = lng.reshape(2, 8, P).transpose(0, 2, 1).reshape(2 * P, 8)
    m["flnb_c"] = lnb.reshape(2, 8, P).transpose(0, 2, 1).reshape(2 * P, 8)
    w1 = np.asarray(inp["L0_ffn_w1"], np.float32)
    w1a = np.zeros((2, E, MSTR * P), np.float32); w1a[:, :, :HID] = w1[:, :, :HID]
    w1g = np.zeros((2, E, MSTR * P), np.float32); w1g[:, :, :HID] = w1[:, :, HID:]
    m["fw1a_s"] = w1a.reshape(2, E, MSTR, P).transpose(0, 2, 1, 3).reshape(2 * MSTR * E, P)
    m["fw1g_s"] = w1g.reshape(2, E, MSTR, P).transpose(0, 2, 1, 3).reshape(2 * MSTR * E, P)
    b1 = np.asarray(inp["L0_ffn_b1"], np.float32)
    b1a = np.zeros((2, MSTR * P), np.float32); b1a[:, :HID] = b1[:, :HID]
    b1g = np.zeros((2, MSTR * P), np.float32); b1g[:, :HID] = b1[:, HID:]
    m["fb1a_c"] = b1a.reshape(2, MSTR, P).transpose(0, 2, 1).reshape(2 * P, MSTR)
    m["fb1g_c"] = b1g.reshape(2, MSTR, P).transpose(0, 2, 1).reshape(2 * P, MSTR)
    w2 = np.asarray(inp["L0_ffn_w2"], np.float32)
    w2p = np.zeros((2, MSTR * P, E), np.float32); w2p[:, :HID, :] = w2
    m["fw2_s"] = w2p.reshape(2, MSTR * P, 8, P).transpose(0, 2, 1, 3).reshape(2 * 8 * MSTR * P, P)
    b2 = np.asarray(inp["L0_ffn_b2"], np.float32)
    m["fb2_c"] = b2.reshape(2, 8, P).transpose(0, 2, 1).reshape(2 * P, 8)
    for k, v in cns.items():
        m["c_" + k] = v
    return {k: np.ascontiguousarray(v) for k, v in m.items()}


def build_program(dumps=(), stages=("A", "B0", "B1", "F"), DBG_SKIP_LN1=False, DBG_HOST_W=False, BARRIERS=True):
    nc = bass.Bass("TRN2", target_bir_lowering=False)
    dumps = set(dumps)
    cns = host_constants()
    laststage = stages[-1]

    def din(name, shape):
        return nc.dram_tensor(name, list(shape), F32, kind="ExternalInput").ap()

    x_h = din("x", [T, E])
    wq_h = din("wq_s", [8 * E, P]); wk_h = din("wk_s", [8 * E, P])
    wv_h = din("wv_s", [4 * E, DV]); wg_h = din("wg_s", [4 * E, DV])
    wo_h = din("wo_s", [8 * 2 * E, P])
    gn_h = din("gn", [1, DV])
    fwqkv_h = din("fwqkv", [2 * E, 3 * FH * DH])
    fqs_h = din("fqs", [2, DH]); fks_h = din("fks", [2, DH])
    fwo_h = din("fwo_s", [2 * 8 * FH * DH, P])
    flng_h = din("flng_c", [2 * P, 8]); flnb_h = din("flnb_c", [2 * P, 8])
    fw1a_h = din("fw1a_s", [2 * MSTR * E, P]); fw1g_h = din("fw1g_s", [2 * MSTR * E, P])
    fb1a_h = din("fb1a_c", [2 * P, MSTR]); fb1g_h = din("fb1g_c", [2 * P, MSTR])
    fw2_h = din("fw2_s", [2 * 8 * MSTR * P, P])
    fb2_h = din("fb2_c", [2 * P, 8])
    dcn = {k: nc.dram_tensor("c_" + k, list(v.shape), F32, kind="ExternalInput").ap()
           for k, v in cns.items()}

    out_h = nc.dram_tensor("out", [T, E], F32, kind="ExternalOutput").ap()
    x1sp_h = nc.dram_tensor("x1spill", [T, E], F32, kind="Internal").ap()
    wsc = {}
    for nm, shape in [("wq", [8 * E, P]), ("wk", [8 * E, P]), ("wv", [4 * E, DV]),
                      ("wg", [4 * E, DV]), ("wo", [8 * 2 * E, P]),
                      ("wqkv", [2 * E, 3 * FH * DH]), ("fwo", [2 * 8 * FH * DH, P]),
                      ("w1a", [2 * MSTR * E, P]), ("w1g", [2 * MSTR * E, P]),
                      ("w2", [2 * 8 * MSTR * P, P])]:
        wsc[nm] = nc.dram_tensor("wsc_" + nm, shape, BF16, kind="Internal").ap()
    dump_shapes = {"qfm": [P, 8 * 512], "kfm": [P, 8 * 512], "vsb": [P, 2 * 2048],
                   "gsb": [P, 2 * 2048], "osb": [P, 2048], "otil": [P, 2048],
                   "ofm": [P, 16 * 256], "aofm": [P, 8 * 512], "x1fm": [P, 8 * 520],
                   "qkhat": [P, 8 * 130], "qhfm": [P, 8 * 512], "psb": [P, 8 * 512],
                   "oall": [P, 4 * 512], "prfm": [P, 8 * 512], "h2fm": [P, 8 * 520],
                   "actfm": [P, MSTR * 512], "kxfm": [P, 4 * FH * 512],
                   "vfsb": [P, 16 * FH * 66]}
    dump_h = {d: nc.dram_tensor("dump_" + d, dump_shapes[d], F32, kind="ExternalOutput").ap()
              for d in dumps}

    rec = Rec()

    with ExitStack() as st:
        def sb(name, shape, dt=F32):
            return st.enter_context(nc.sbuf_tensor(name, shape, dt))

        def pst(name, shape, dt=F32):
            return st.enter_context(nc.psum_tensor(name, shape, dt))

        strm = sb("strm", [P, CPP * E])

        AB_N = 35584
        ab = sb("ab", [P, AB_N], BF16)
        pos = [0]

        def carve(nel):
            o = pos[0]; pos[0] += nel
            assert pos[0] <= AB_N, f"arena overflow {pos[0]}"
            return ab[:, o:o + nel]

        a0 = pos[0]
        xfm = carve(8 * 512)
        qfm = carve(8 * 512)
        kfm = carve(8 * 512)
        vsb = carve(2 * 2048)
        gsb = carve(2 * 2048)
        otil = carve(2048)
        ofm = carve(16 * 256)
        aofm = carve(8 * 512)
        A_END = pos[0]
        pos[0] = a0
        x1fm = carve(8 * 520)
        qhfm = carve(8 * 512)
        psb = carve(8 * 512)
        qkhat = carve(8 * 130)
        h2fm = carve(8 * 520)
        actfm = carve(MSTR * 512)
        prfm = carve(8 * 512)
        oall = carve(4 * 512)
        rbc = carve(512)
        B_END = pos[0]
        assert max(A_END, B_END) <= AB_N, (A_END, B_END)

        A_TOK = ["A:xfm", "A:qfm", "A:kfm", "A:vsb", "A:gsb", "A:otil", "A:ofm", "A:aofm"]
        B_TOK = ["B:x1fm", "B:qhfm", "B:psb", "B:qkhat", "B:h2fm", "B:actfm",
                 "B:prfm", "B:oall", "B:rbc"]
        RO_A = ["b:rr0", "b:rr512", "b:rr1024", "b:rr1536"]
        RO_B = ["b:ro1", "b:ro2"]
        ALIAS_TOK = RO_A + RO_B + ["b:osb", "b:x1rl"]
        ALL_TOK = None

        W_N = 14336
        wbuf = sb("wbuf", [P, W_N], BF16)
        wqk_r = [wbuf[:, i * 1024:(i + 1) * 1024] for i in range(2)]
        wvg_r = [wbuf[:, 2048 + i * 4096:2048 + (i + 1) * 4096] for i in range(2)]
        wo_r = [wbuf[:, 10240 + i * 2048:10240 + (i + 1) * 2048] for i in range(2)]
        wqkv_res = wbuf[:, 0:8 * 1536]
        wfo_r = [wbuf[:, 12288 + i * 512:12288 + (i + 1) * 512] for i in range(2)]
        w1a_r = [wbuf[:, i * 1024:(i + 1) * 1024] for i in range(2)]
        w1g_r = [wbuf[:, 2048 + i * 1024:2048 + (i + 1) * 1024] for i in range(2)]
        w2_r = [wbuf[:, 4096 + i * 2816:4096 + (i + 1) * 2816] for i in range(2)]
        WA_TOK = ["W:qk0", "W:qk1", "W:vg0", "W:vg1", "W:wo0", "W:wo1"]
        WB1_TOK = ["W:qkv", "W:fo0", "W:fo1"]
        WB2_TOK = ["W:w1a0", "W:w1a1", "W:w1g0", "W:w1g1", "W:w20", "W:w21"]
        ALL_ARENA = (A_TOK + B_TOK + WA_TOK + WB1_TOK + WB2_TOK + ALIAS_TOK)

        Sbf = sb("Sbf", [P, 2 * RH * 2 * 512], BF16)
        kxfm = sb("kxfm", [P, 2 * 2 * FH * 512], BF16)
        vfsb = sb("vfsb", [P, 2 * 2 * 4 * FH * 66], BF16)
        xbf = sb("xbf", [P, E], BF16)
        sqs = sb("sqs", [P, E], BF16)
        atil = sb("atil", [P, P], BF16)
        ktil = sb("ktil", [P, 256], BF16)
        qkc1 = sb("qkc1", [P, 128], BF16)
        qkc2 = sb("qkc2", [P, 128], BF16)
        m8n = sb("m8n", [P, FH], BF16)
        lbbc = sb("lbbc", [P, 8], BF16)
        onesr = sb("onesr", [P, P], BF16)
        ones32 = sb("ones32", [P, P])
        epscol = sb("epscol", [P, 2])
        ident = sb("ident", [P, P], BF16)

        rotr = sb("rotr", [P, 2048])
        osb = rotr
        rcos = sb("rcos", [P, 4 * 512], BF16)
        qkct = sb("qkct", [P, 128])
        stats = sb("stats", [P, 16])
        statsb = sb("statsb", [P, 16])
        abcol = sb("abcol", [P, 16])
        stat = sb("stat", [P, 16])
        ssq = sb("ssq", [P, 48])
        rst = sb("rst", [P, 64])
        diag = sb("diag", [P, FH])
        m8 = sb("m8", [P, FH])
        rsum = sb("rsum", [P, 512])
        b2c = sb("b2c", [P, 8])
        carry = sb("carry", [P, 48])
        carrys = sb("carrys", [P, 8])
        gnrep = sb("gnrep", [P, DV], BF16)
        qsrep = sb("qsrep", [P, 2 * DH])
        lggc = sb("lggc", [P, 8])
        bias1a = sb("bias1a", [P, 2 * MSTR])
        bias1g = sb("bias1g", [P, 2 * MSTR])
        bias2 = sb("bias2", [P, 16])
        mtil = sb("mtil", [P, RH * P], BF16)
        gcolt = sb("gcolt", [P, RH])
        gcolks = sb("gcolks", [P, RH])
        tric = sb("tric", [P, P], BF16)
        trif = sb("trif", [P, P], BF16)
        shdn = sb("shdn", [P, P]); shdnb = sb("shdnb", [P, P])
        shup = sb("shup", [P, P]); shupb = sb("shupb", [P, P])

        pb = [pst(f"pb{i}", [P, 512]) for i in range(6)]
        pT1 = pst("pT1", [P, 1024], BF16)
        pT2 = pst("pT2", [P, 1024], BF16)

        sems = {e: st.enter_context(nc.semaphore("s_" + e)) for e in ENGINES}
        block = st.enter_context(nc.Block())

        def dma(eng, out, in_, reads, writes, tag=""):
            return rec.add(eng, lambda q, out=out, in_=in_: q.dma_start(out=out, in_=in_),
                           reads=reads, writes=writes, tag=tag)

        op = rec.add

        wq_rr = [0]

        def wload(slot, hsl, ssl, tok, stok, p, kc, tag, DBG_HOST_W=DBG_HOST_W):
            sview = slot.rearrange("p (kc m) -> p kc m", kc=kc)
            if p == 0:
                dma("gps", sview, hsl.rearrange("(kc p) m -> p kc m", kc=kc),
                    [tok], [tok], tag)
                dma("sync", ssl.rearrange("(kc p) m -> p kc m", kc=kc), sview,
                    [tok], [stok], tag + "sc")
            elif DBG_HOST_W:
                dma("gps", sview, hsl.rearrange("(kc p) m -> p kc m", kc=kc),
                    [tok], [tok], tag)
            else:
                wq_rr[0] ^= 1
                dma("gps" if wq_rr[0] else "sync", sview,
                    ssl.rearrange("(kc p) m -> p kc m", kc=kc),
                    [stok, tok], [tok], tag)

        def dump(name, ap, rtok):
            if name in dump_h:
                dma("gps", dump_h[name], ap, rtok, [f"o:d{name}"], "dump")
                dump_done.append(f"o:d{name}")
        dump_done = []

        dma("gps", mtil[:], dcn["ret_mtil"], [], ["c:mtil"])
        dma("sync", gcolt[:], dcn["ret_gcol_t"], [], ["c:gct"])
        dma("sync", gcolks[:], dcn["ret_gcol_ks"], [], ["c:gcks"])
        dma("gps", tric[:], dcn["tri_causal"], [], ["c:tric"])
        dma("gps", trif[:], dcn["tri_far"], [], ["c:trif"])
        dma("sync", shdn[:], dcn["shift_dn"], [], ["c:shdn"])
        dma("sync", shdnb[:], dcn["shift_dnb"], [], ["c:shdnb"])
        dma("sync", shup[:], dcn["shift_up"], [], ["c:shup"])
        dma("sync", shupb[:], dcn["shift_upb"], [], ["c:shupb"])
        dma("gps", gnrep[:], gn_h.partition_broadcast(P).squeeze(1), [], ["c:gnrep"])
        dma("gps", ident[:], dcn["ident"], [], ["c:ident"])
        op("dve", lambda v: v.memset(onesr[:], 1.0), [], ["c:onesr"])
        op("dve", lambda v: v.memset(ones32[:], 1.0), [], ["c:ones32"])
        op("dve", lambda v: v.memset(epscol[:], 1e-5), [], ["c:eps"])
        op("dve", lambda v: v.memset(Sbf[:], 0.0), [], ["st:S0", "st:S1"])
        op("dve", lambda v: v.memset(carry[:], 0.0), [], ["st:carry0", "st:carry1", "st:carryh0", "st:carryh1"])
        op("dve", lambda v: v.memset(carrys[:], 0.0), [], ["st:carrys0", "st:carrys1"])
        op("dve", lambda v: v.memset(kxfm[:], 0.0), [], ["st:kx00", "st:kx01", "st:kx10", "st:kx11"])
        op("dve", lambda v: v.memset(vfsb[:], 0.0), [], ["st:vf00", "st:vf01", "st:vf10", "st:vf11"])

        def vf_ones(v):
            ap = vfsb[:].rearrange("p (w h c) -> p (w h) c", w=16, h=FH)[:, :, 64:65]
            return v.memset(ap, 1.0)
        op("dve", vf_ones, ["st:vf00", "st:vf01", "st:vf10", "st:vf11"],
           ["st:vf00", "st:vf01", "st:vf10", "st:vf11"])

        state = {"Sp": 0, "layer": None}

        def emit_ln(c, tag):
            chunk = strm[:, c * E:(c + 1) * E]
            sc = f"s:strm{c}"
            op("act", lambda s, chunk=chunk: s.activation(
                sqs[:], chunk, AF.Square, accum_out=stat[:, 1:2]),
               [sc], ["b:st1", "b:sqs"], tag + "sq")
            op("dve", lambda v: v.tensor_scalar_mul(stat[:, 2:3], stat[:, 0:1], 1.0 / E),
               ["b:st0"], ["b:st2"], tag + "mu")
            op("dve", lambda v: v.tensor_tensor(stat[:, 3:4], stat[:, 2:3], stat[:, 2:3], ALU.mult),
               ["b:st2"], ["b:st3"], tag + "mu2")
            op("dve", lambda v: v.tensor_scalar_mul(stat[:, 9:10], stat[:, 1:2], 1.0 / E),
               ["b:st1"], ["b:st9"], tag + "msq")
            op("dve", lambda v: v.tensor_tensor(stat[:, 4:5], stat[:, 9:10], stat[:, 3:4],
                                                ALU.subtract),
               ["b:st9", "b:st3"], ["b:st4"], tag + "var")
            op("act", lambda s: s.activation(stat[:, 5:6], stat[:, 4:5], AF.Sqrt,
                                             bias=epscol[:, 0:1]),
               ["b:st4", "c:eps"], ["b:st5"], tag + "sd")
            op("dve", lambda v: v.reciprocal(stat[:, 6:7], stat[:, 5:6]),
               ["b:st5"], ["b:st6"], tag + "rs")
            op("dve", lambda v: v.tensor_tensor(stat[:, 7:8], stat[:, 2:3], stat[:, 6:7], ALU.mult),
               ["b:st2", "b:st6"], ["b:st7"], tag + "nb")
            op("dve", lambda v: v.tensor_scalar_mul(stat[:, 8:9], stat[:, 7:8], -1.0),
               ["b:st7"], ["b:st8"], tag + "ng")
            op("dve", lambda v, chunk=chunk: v.tensor_scalar(
                chunk, chunk, stat[:, 6:7], stat[:, 8:9], ALU.mult, ALU.add),
               [sc, "b:st6", "b:st8"], [sc], tag + "ap")

        def stageA(p):
            BARRIERS and rec.barrier(f"bA{p}_")
            rec.fence([], ALL_ARENA, f"fA{p}")
            t0 = p * 512
            dma("gps", rcos[:].rearrange("p (g c) -> p g c", g=4),
                dcn["ret_rot4"][:, :, t0:t0 + 512], ["c:rcos"], ["c:rcos"], "rcos")
            dma("sync", strm[:].rearrange("p (c e) -> p c e", c=CPP),
                x_h[p * 512:(p + 1) * 512, :].rearrange("(c p) e -> p c e", c=CPP),
                [f"s:strm{c}" for c in range(CPP)],
                [f"s:strm{c}" for c in range(CPP)], "xload")
            for c in range(CPP):
                gc = p * CPP + c
                dma("gps", xbf[:], x_h[gc * P:(gc + 1) * P, :], ["b:xbf"], ["b:xbf"], "xloadb")

                def xtr(t):
                    last = None
                    for z in range(8):
                        last = t.transpose(pT1[:, z * P:(z + 1) * P],
                                           xbf[:, z * P:(z + 1) * P], ident[:])
                    return last
                op("pe", xtr, ["b:xbf", "c:ident", "p:T1"], ["p:T1"], f"xT{c}")

                def xev(v, c=c):
                    dst = xfm.rearrange("p (e c) -> p e c", e=8)[:, :, c * P:(c + 1) * P]
                    src = pT1[:].rearrange("p (e c) -> p e c", e=8)
                    return v.tensor_copy(dst, src)
                op("dve", xev, ["p:T1"], ["A:xfm"], f"xE{c}")
            for qk in range(2):
                wh = wq_h if qk == 0 else wk_h
                dst = qfm if qk == 0 else kfm
                dtok = "A:qfm" if qk == 0 else "A:kfm"
                co = 0 if qk == 0 else 1024
                for dkc in range(8):
                    sl = dkc % 2
                    wt = wqk_r[sl]
                    wload(wt, wh[dkc * E:(dkc + 1) * E, :],
                          wsc["wq" if qk == 0 else "wk"][dkc * E:(dkc + 1) * E, :],
                          f"W:qk{sl}", f"Wsc:qk{qk}{dkc}", p, 8, "wqk")
                    bank = pb[sl] if qk == 0 else pb[2 + sl]
                    bt = f"p:b{sl}" if qk == 0 else f"p:b{2+sl}"

                    def mm(t, wt=wt, bank=bank):
                        last = None
                        for kc in range(8):
                            last = t.matmul(bank[:], wt[:, kc * P:(kc + 1) * P],
                                            xfm.rearrange("p (e c) -> p e c", e=8)[:, kc, :],
                                            start=(kc == 0), stop=(kc == 7))
                        return last
                    op("pe", mm, [f"W:qk{sl}", "A:xfm", bt], [bt], f"qkp{qk}{dkc}")
                    o1 = 0 if dkc % 2 == 0 else 1024

                    def r1(v, bank=bank, o1=o1, co=co):
                        return v.tensor_tensor(rotr[:, o1:o1 + 512], bank[:],
                                               rcos[:, co:co + 512], ALU.mult)

                    def r2(v, bank=bank, o1=o1, co=co):
                        return v.tensor_tensor(rotr[:, o1 + 512:o1 + 1024], bank[:],
                                               rcos[:, co + 512:co + 1024], ALU.mult)
                    op("dve", r1, [bt, "c:rcos"], [f"b:rr{o1}"], f"r1{qk}{dkc}")
                    op("dve", r2, [bt, "c:rcos"], [f"b:rr{o1+512}"], f"r2{qk}{dkc}")
                    if dkc % 2 == 1:
                        h = dkc // 2

                        def cA(v, dst=dst, h=h):
                            return v.tensor_tensor(dst[:, (2 * h) * 512:(2 * h) * 512 + 512],
                                                   rotr[:, 0:512], rotr[:, 1536:2048], ALU.subtract)

                        def cB(v, dst=dst, h=h):
                            return v.tensor_tensor(dst[:, (2 * h + 1) * 512:(2 * h + 1) * 512 + 512],
                                                   rotr[:, 1024:1536], rotr[:, 512:1024], ALU.add)
                        rr = ["b:rr0", "b:rr512", "b:rr1024", "b:rr1536"]
                        op("dve", cA, rr, [dtok], f"cA{qk}{h}")
                        op("dve", cB, rr, [dtok], f"cB{qk}{h}")
            if p == 0:
                dump("qfm", qfm, ["A:qfm"])
                dump("kfm", kfm, ["A:kfm"])
            for half in range(2):
              chs = [2 * half, 2 * half + 1]
              for vg in range(2):
                wh = wv_h if vg == 0 else wg_h
                dtok = "A:vsb" if vg == 0 else "A:gsb"
                for nb in range(4):
                    sl = nb % 2
                    wt = wvg_r[sl]
                    wload(wt, wh[nb * E:(nb + 1) * E, :],
                          wsc["wv" if vg == 0 else "wg"][nb * E:(nb + 1) * E, :],
                          f"W:vg{sl}", f"Wsc:vg{vg}{nb}", p if half == 0 else 1, 8, "wvg")
                    for lc, c in enumerate(chs):
                        bank = pb[4 + lc]
                        bt = f"p:b{4 + lc}"

                        def mm(t, wt=wt, c=c, bank=bank):
                            last = None
                            for kc in range(8):
                                last = t.matmul(
                                    bank[:],
                                    xfm.rearrange("p (e c) -> p e c", e=8)[:, kc, c * P:(c + 1) * P],
                                    wt[:, kc * 512:(kc + 1) * 512],
                                    start=(kc == 0), stop=(kc == 7))
                            return last
                        op("pe", mm, [f"W:vg{sl}", "A:xfm", bt], [bt], f"vg{vg}{nb}{c}")
                        if vg == 0:
                            def ev(v, lc=lc, nb=nb, bank=bank):
                                return v.tensor_copy(
                                    vsb[:, lc * 2048 + nb * 512:lc * 2048 + (nb + 1) * 512], bank[:])
                        else:
                            def ev(v, lc=lc, nb=nb, bank=bank):
                                return v.tensor_tensor(
                                    gsb[:, lc * 2048 + nb * 512:lc * 2048 + (nb + 1) * 512],
                                    bank[:], gnrep[:], ALU.mult)
                        op("dve", ev, [bt, "c:gnrep"], [dtok], f"vge{vg}{nb}{c}")
              if half == 0:
                  rec.fence([], ALIAS_TOK, f"fR{p}")
              for lc, c in enumerate(chs):
                for h in range(RH):
                    Sp = state["Sp"]
                    hb = h % 2
                    ab = 0 if hb == 0 else 4
                    obk = 1 if hb == 0 else 5
                    atl = atil[:] if hb == 0 else xbf[:, 256:384]
                    atok = "b:atil" if hb == 0 else "b:xbf"
                    ktl = ktil[:] if hb == 0 else xbf[:, 0:256]
                    ktok = "b:ktil" if hb == 0 else "b:xbf"

                    def amm(t, h=h, c=c, ab=ab):
                        last = None
                        for z in range(2):
                            last = t.matmul(
                                pb[ab][:, 0:128],
                                kfm[:, (2 * h + z) * 512 + c * P:(2 * h + z) * 512 + (c + 1) * P],
                                qfm[:, (2 * h + z) * 512 + c * P:(2 * h + z) * 512 + (c + 1) * P],
                                start=(z == 0), stop=(z == 1))
                        return last
                    op("pe", amm, ["A:kfm", "A:qfm", f"p:b{ab}"], [f"p:b{ab}"], f"amm{h}{c}")
                    op("dve", lambda v, h=h, ab=ab, atl=atl: v.tensor_tensor(
                        atl, pb[ab][:, 0:128], mtil[:, h * P:(h + 1) * P], ALU.mult),
                       [f"p:b{ab}", "c:mtil"], [atok], f"atf{h}{c}")

                    def ktr(t, h=h, c=c, hb=hb):
                        last = None
                        for z in range(2):
                            last = t.transpose(
                                pT2[:, hb * 256 + z * P:hb * 256 + (z + 1) * P],
                                kfm[:, (2 * h + z) * 512 + c * P:(2 * h + z) * 512 + (c + 1) * P],
                                ident[:])
                        return last
                    op("pe", ktr, ["A:kfm", "c:ident", "p:T2"], ["p:T2"], f"ktr{h}{c}")
                    op("act", lambda s, h=h, hb=hb, ktl=ktl: s.activation(
                        ktl, pT2[:, hb * 256:hb * 256 + 256], AF.Identity,
                        scale=gcolks[:, h:h + 1], bias=0.0),
                       ["p:T2", "c:gcks"], [ktok], f"kev{h}{c}")

                    def omm(t, h=h, c=c, lc=lc, Sp=Sp, obk=obk, atl=atl):
                        t.matmul(pb[obk][:], atl,
                                 vsb[:, lc * 2048 + h * 512:lc * 2048 + (h + 1) * 512],
                                 start=True, stop=False)
                        last = None
                        for z in range(2):
                            last = t.matmul(
                                pb[obk][:],
                                qfm[:, (2 * h + z) * 512 + c * P:(2 * h + z) * 512 + (c + 1) * P],
                                Sbf[:, (Sp * RH * 2 + h * 2 + z) * 512:(Sp * RH * 2 + h * 2 + z + 1) * 512],
                                start=False, stop=(z == 1))
                        return last
                    op("pe", omm, [atok, "A:vsb", "A:qfm", f"st:S{Sp}", f"p:b{obk}"],
                       [f"p:b{obk}"], f"omm{h}{c}")
                    op("act", lambda s, h=h, obk=obk: s.activation(
                        osb[:, h * 512:(h + 1) * 512], pb[obk][:], AF.Identity,
                        scale=gcolt[:, h:h + 1], bias=0.0),
                       [f"p:b{obk}", "c:gct"], ["b:osb"], f"oev{h}{c}")

                    def dsm(t, h=h, lc=lc, ktl=ktl):
                        last = None
                        for z in range(2):
                            last = t.matmul(pb[2 + z][:], ktl[:, z * P:(z + 1) * P],
                                            vsb[:, lc * 2048 + h * 512:lc * 2048 + (h + 1) * 512],
                                            start=True, stop=True)
                        return last
                    op("pe", dsm, [ktok, "A:vsb", "p:b2", "p:b3"], ["p:b2", "p:b3"],
                       f"dsm{h}{c}")
                    for z in range(2):
                        def su(v, h=h, z=z, Sp=Sp):
                            src = Sbf[:, (Sp * RH * 2 + h * 2 + z) * 512:(Sp * RH * 2 + h * 2 + z + 1) * 512]
                            dst = Sbf[:, ((1 - Sp) * RH * 2 + h * 2 + z) * 512:((1 - Sp) * RH * 2 + h * 2 + z + 1) * 512]
                            return v.scalar_tensor_tensor(dst, in0=src, scalar=GAMMAS[h] ** P,
                                                          in1=pb[2 + z][:], op0=ALU.mult, op1=ALU.add)
                        op("dve", su, [f"st:S{Sp}", f"p:b{2+z}"], [f"st:S{1-Sp}"], f"su{h}{c}{z}")
                state["Sp"] = 1 - state["Sp"]
                for h in range(RH):
                    op("act", lambda s, h=h: s.activation(
                        sqs[:, 0:512], osb[:, h * 512:(h + 1) * 512], AF.Square,
                        accum_out=ssq[:, 32 + h:33 + h]),
                       ["b:osb", "b:sqs"], ["b:ssq", "b:sqs"], f"rsq{h}{c}")
                op("dve", lambda v: v.tensor_scalar_mul(ssq[:, 40:40 + RH], ssq[:, 32:32 + RH],
                                                        1.0 / DV),
                   ["b:ssq"], ["b:ssq2"], f"rmsc{c}")
                op("act", lambda s: s.activation(rst[:, 48:48 + RH], ssq[:, 40:40 + RH], AF.Sqrt,
                                                 bias=epscol[:, 0:1]),
                   ["b:ssq2", "c:eps"], ["b:rst"], f"rms{c}")
                op("dve", lambda v: v.reciprocal(rst[:, 52:52 + RH], rst[:, 48:48 + RH]),
                   ["b:rst"], ["b:rst2"], f"rrc{c}")
                op("dve", lambda v, lc=lc: v.tensor_tensor(
                    otil[:], osb[:], gsb[:, lc * 2048:(lc + 1) * 2048], ALU.mult),
                   ["b:osb", "A:gsb"], ["A:otil"], f"og{c}")

                def ot2(v):
                    src = otil.rearrange("p (h d) -> p h d", h=RH)
                    rsv = rst[:, 52:52 + RH].unsqueeze(2).broadcast_to([P, RH, DV])
                    return v.tensor_tensor(src, src, rsv, ALU.mult)
                op("dve", ot2, ["A:otil", "b:rst2"], ["A:otil"], f"or{c}")
                for hf in range(2):
                    def otr(t, hf=hf):
                        last = None
                        for z in range(8):
                            ic = hf * 8 + z
                            last = t.transpose(pT1[:, z * P:(z + 1) * P],
                                               otil[:, ic * P:(ic + 1) * P], ident[:])
                        return last
                    op("pe", otr, ["A:otil", "c:ident", "p:T1"], ["p:T1"], f"oT{c}{hf}")

                    def oev2(v, lc=lc, hf=hf):
                        dst = ofm.rearrange("p (i c) -> p i c", i=16)[:, hf * 8:(hf + 1) * 8,
                                                                     lc * P:(lc + 1) * P]
                        src = pT1[:].rearrange("p (i c) -> p i c", i=8)
                        return v.tensor_copy(dst, src)
                    op("dve", oev2, ["p:T1"], ["A:ofm"], f"oE{c}{hf}")
              if p == 0 and half == 0:
                  dump("vsb", vsb, ["A:vsb"])
                  dump("gsb", gsb, ["A:gsb"])
                  dump("osb", osb[:], ["b:osb"])
                  dump("otil", otil, ["A:otil"])
                  dump("ofm", ofm, ["A:ofm"])
              for ec in range(8):
                sl = ec % 2
                wt = wo_r[sl]
                wload(wt, wo_h[ec * 2 * E:(ec + 1) * 2 * E, :],
                      wsc["wo"][ec * 2 * E:(ec + 1) * 2 * E, :],
                      f"W:wo{sl}", f"Wsc:wo{ec}", p if half == 0 else 1, 16, "wwo")
                bank = pb[ec % 4]
                bt = f"p:b{ec % 4}"

                def mm(t, wt=wt, bank=bank):
                    last = None
                    for kc in range(16):
                        last = t.matmul(bank[:, 0:256], wt[:, kc * P:(kc + 1) * P],
                                        ofm.rearrange("p (i c) -> p i c", i=16)[:, kc, :],
                                        start=(kc == 0), stop=(kc == 15))
                    return last
                op("pe", mm, [f"W:wo{sl}", "A:ofm", bt], [bt], f"wo{half}{ec}")
                op("dve", lambda v, ec=ec, bank=bank, half=half: v.tensor_copy(
                    aofm[:, ec * 512 + half * 256:ec * 512 + half * 256 + 256], bank[:, 0:256]),
                   [bt], ["A:aofm"], f"woe{half}{ec}")
            if p == 0:
                dump("aofm", aofm, ["A:aofm"])
            for c in range(CPP):
                def atr(t, c=c):
                    last = None
                    for ec in range(8):
                        last = t.transpose(pT2[:, ec * P:(ec + 1) * P],
                                           aofm[:, ec * 512 + c * P:ec * 512 + (c + 1) * P],
                                           ident[:])
                    return last
                op("pe", atr, ["A:aofm", "c:ident", "p:T2"], ["p:T2"], f"aT{c}")
                op("dve", lambda v, c=c: v.scalar_tensor_tensor(
                    strm[:, c * E:(c + 1) * E], in0=pT2[:], scalar=1.0,
                    in1=strm[:, c * E:(c + 1) * E], op0=ALU.mult, op1=ALU.add,
                    accum_out=stat[:, 0:1]),
                   ["p:T2", f"s:strm{c}"], [f"s:strm{c}", "b:st0"], f"res{c}")
                if not DBG_SKIP_LN1:
                    emit_ln(c, f"ln1_{p}_{c}")
                gc = p * CPP + c
                dma("sync", x1sp_h[gc * P:(gc + 1) * P, :], strm[:, c * E:(c + 1) * E],
                    [f"s:strm{c}"], [f"sp:{gc}"], "x1sp")

        def stageB_attn(p, l):
            BARRIERS and rec.barrier(f"bB{p}{l}_")
            rec.fence([], ALL_ARENA, f"fB{p}{l}")
            par = p % 2
            dma("sync", qsrep[:, 0:DH], fqs_h[l:l + 1, :].partition_broadcast(P).squeeze(1),
                ["c:qsr"], ["c:qsr"], "qs")
            dma("sync", qsrep[:, DH:2 * DH], fks_h[l:l + 1, :].partition_broadcast(P).squeeze(1),
                ["c:qsr"], ["c:qsr"], "ks")
            dma("sync", lggc[:], flng_h[l * P:(l + 1) * P, :], ["c:lgg"], ["c:lgg"], "lgg")
            dma("gps", lbbc[:], flnb_h[l * P:(l + 1) * P, :], ["c:lbb"], ["c:lbb"], "lbb")
            dma("sync", bias2[:, l * 8:(l + 1) * 8], fb2_h[l * P:(l + 1) * P, :],
                [f"c:b2_{l}"], [f"c:b2_{l}"], "b2")
            op("dve", lambda v: v.reduce_max(b2c[:, 0:1], qsrep[:, 0:DH], axis=AX.X,
                                             apply_absolute_value=True),
               ["c:qsr"], ["b:b2a"], "mxq")
            op("dve", lambda v: v.reduce_max(b2c[:, 1:2], qsrep[:, DH:2 * DH], axis=AX.X,
                                             apply_absolute_value=True),
               ["c:qsr"], ["b:b2b"], "mxk")
            op("dve", lambda v: v.tensor_tensor(b2c[:, 2:3], b2c[:, 0:1], b2c[:, 1:2], ALU.mult),
               ["b:b2a", "b:b2b"], ["b:b2c"], "mxm")
            op("dve", lambda v: v.tensor_scalar(b2c[:, 3:4], b2c[:, 2:3], QKS, -20.0,
                                                ALU.mult, ALU.add),
               ["b:b2c"], ["b:b2f"], "mxf")
            for kc in range(8):
                wload(wqkv_res[:, kc * 1536:(kc + 1) * 1536],
                      fwqkv_h[l * E + kc * P:l * E + (kc + 1) * P, :],
                      wsc["wqkv"][l * E + kc * P:l * E + (kc + 1) * P, :],
                      "W:qkv", f"Wsc:qkv{l}{kc}", p, 1, "wqkv")
            for c in range(CPP):
                op("dve", lambda v, c=c: v.tensor_copy(xbf[:], strm[:, c * E:(c + 1) * E]),
                   [f"s:strm{c}", "b:xbf"], ["b:xbf"], f"x1b{c}")

                def xtr(t):
                    last = None
                    for z in range(8):
                        last = t.transpose(pT1[:, z * P:(z + 1) * P],
                                           xbf[:, z * P:(z + 1) * P], ident[:])
                    return last
                op("pe", xtr, ["b:xbf", "c:ident", "p:T1"], ["p:T1"], f"x1T{c}")

                def xev(v, c=c):
                    dst = x1fm.rearrange("p (e c) -> p e c", e=8)[:, :, 4 + c * P:4 + (c + 1) * P]
                    src = pT1[:].rearrange("p (e c) -> p e c", e=8)
                    return v.tensor_copy(dst, src)
                op("dve", xev, ["p:T1"], ["B:x1fm"], f"x1E{c}")
            def brst(v, l=l):
                dst = x1fm.rearrange("p (e c) -> p e c", e=8)[:, :, 3:4]
                return v.tensor_copy(dst, carry[:, l * 8:l * 8 + 8].unsqueeze(2))
            op("dve", brst, [f"st:carry{l}", "B:x1fm"], ["B:x1fm"], "brst")

            def bsav(v, l=l):
                src = x1fm.rearrange("p (e c) -> p e c", e=8)[:, :, 515:516]
                return v.tensor_copy(carry[:, l * 8:l * 8 + 8].unsqueeze(2), src)
            op("dve", bsav, ["B:x1fm"], [f"st:carry{l}"], "bsav")

            for c in range(CPP):
                gc = p * CPP + c
                dma("sync", qkct[:], dcn["ffn_ct"][gc * P:(gc + 1) * P, :], ["c:qkct"],
                    ["c:qkct"], "fct")
                def bq1(v):
                    src = qkct[:, 0:32].unsqueeze(1).broadcast_to([P, 2, 32])
                    qs = qsrep[:, 0:DH].rearrange("p (two j) -> p two j", two=2)
                    return v.tensor_tensor(qkc1[:, 0:DH].rearrange("p (two j) -> p two j", two=2),
                                           src, qs, ALU.mult)
                def bk1(v):
                    src = qkct[:, 64:96].unsqueeze(1).broadcast_to([P, 2, 32])
                    ks = qsrep[:, DH:2 * DH].rearrange("p (two j) -> p two j", two=2)
                    return v.tensor_tensor(qkc1[:, DH:2 * DH].rearrange("p (two j) -> p two j", two=2),
                                           src, ks, ALU.mult)
                def bq2(v):
                    src = qkct[:, 32:64].unsqueeze(1).broadcast_to([P, 2, 32])
                    qs = qsrep[:, 0:DH].rearrange("p (two j) -> p two j", two=2)
                    return v.tensor_tensor(qkc2[:, 0:DH].rearrange("p (two j) -> p two j", two=2),
                                           src, qs, ALU.mult)
                def bk2(v):
                    src = qkct[:, 96:128].unsqueeze(1).broadcast_to([P, 2, 32])
                    ks = qsrep[:, DH:2 * DH].rearrange("p (two j) -> p two j", two=2)
                    return v.tensor_tensor(qkc2[:, DH:2 * DH].rearrange("p (two j) -> p two j", two=2),
                                           src, ks, ALU.mult)
                op("dve", bq1, ["c:qkct", "c:qsr"], ["c:qkc1"], "bq1")
                op("dve", bk1, ["c:qkct", "c:qsr"], ["c:qkc1"], "bk1")
                op("dve", bq2, ["c:qkct", "c:qsr"], ["c:qkc2"], "bq2")
                op("dve", bk2, ["c:qkct", "c:qsr"], ["c:qkc2"], "bk2")
                cb = 3 * (c % 2)
                for nb in range(3):
                    def mm(t, c=c, nb=nb, cb=cb):
                        last = None
                        for kc in range(8):
                            colofs = (4 + c * P) if kc < 4 else (3 + c * P)
                            last = t.matmul(
                                pb[cb + nb][:],
                                x1fm.rearrange("p (e c) -> p e c", e=8)[:, kc, colofs:colofs + P],
                                wqkv_res[:, kc * 1536 + nb * 512:kc * 1536 + (nb + 1) * 512],
                                start=(kc == 0), stop=(kc == 7))
                        return last
                    op("pe", mm, ["B:x1fm", "W:qkv", f"p:b{cb+nb}"], [f"p:b{cb+nb}"], f"qkv{c}{nb}")
                def vev(v, par=par, l=l, c=c, cb=cb):
                    dst = vfsb[:].rearrange("p (w ch h cc) -> p w ch h cc", w=4, ch=4,
                                            h=FH)[:, l * 2 + par, c, :, 0:64]
                    src = pb[cb + 2][:].rearrange("p (h d) -> p h d", h=FH)
                    return v.tensor_copy(dst, src)
                op("dve", vev, [f"p:b{cb+2}"], [f"st:vf{l}{par}"], f"vev{c}")
                qks = sqs if c % 2 == 0 else xbf
                sqscr = xbf if c % 2 == 0 else sqs
                qtok = "b:sqs" if c % 2 == 0 else "b:xbf"
                stok2 = "b:xbf" if c % 2 == 0 else "b:sqs"
                so = 16 * (c % 2)
                op("dve", lambda v, cb=cb, qks=qks: v.tensor_copy(qks[:, 0:512], pb[cb][:]),
                   [f"p:b{cb}", qtok], [qtok], f"qe{c}")
                op("dve", lambda v, cb=cb, qks=qks: v.tensor_copy(qks[:, 512:1024], pb[cb + 1][:]),
                   [f"p:b{cb+1}", qtok], [qtok], f"ke{c}")
                op("act", lambda s, qks=qks, sqscr=sqscr: s.activation(sqscr[:], qks[:], AF.Square),
                   [qtok, stok2], [stok2], f"sq{c}")
                op("dve", lambda v, sqscr=sqscr, so=so: v.reduce_sum(
                    ssq[:, so:so + 16], sqscr[:].rearrange("p (g d) -> p g d", g=16), axis=AX.X),
                   [stok2], [f"b:ssq{c%2}"], f"ss{c}")
                op("act", lambda s, so=so: s.activation(rst[:, so:so + 16], ssq[:, so:so + 16],
                                                        AF.Sqrt),
                   [f"b:ssq{c%2}"], [f"b:rst{c%2}"], f"sr{c}")
                op("dve", lambda v, so=so: v.tensor_scalar_max(rst[:, 32 + so:so + 48],
                                                               rst[:, so:so + 16], 1e-12),
                   [f"b:rst{c%2}"], [f"b:rstm{c%2}"], f"sm{c}")
                op("dve", lambda v, so=so: v.reciprocal(rst[:, so:so + 16], rst[:, 32 + so:so + 48]),
                   [f"b:rstm{c%2}"], [f"b:rst{c%2}"], f"si{c}")
                def r1(v, qks=qks):
                    i1 = qkc1[:].rearrange("p (qk d) -> p qk d", qk=2)\
                        .unsqueeze(2).broadcast_to([P, 2, FH, DH])
                    return v.tensor_tensor(
                        rotr[:, 0:1024].rearrange("p (qk h d) -> p qk h d", qk=2, h=FH),
                        qks[:].rearrange("p (qk h d) -> p qk h d", qk=2, h=FH), i1, ALU.mult)
                def r2(v, qks=qks):
                    i1 = qkc2[:].rearrange("p (qk d) -> p qk d", qk=2)\
                        .unsqueeze(2).broadcast_to([P, 2, FH, DH])
                    return v.tensor_tensor(
                        rotr[:, 1024:2048].rearrange("p (qk h d) -> p qk h d", qk=2, h=FH),
                        qks[:].rearrange("p (qk h d) -> p qk h d", qk=2, h=FH), i1, ALU.mult)
                op("dve", r1, [qtok, "c:qkc1"], ["b:ro1"], f"r1{c}")
                op("dve", r2, [qtok, "c:qkc2"], ["b:ro2"], f"r2{c}")
                qkh = qkhat.rearrange("p (h w) -> p h w", h=FH)
                r1v = rotr[:, 0:1024].rearrange("p (qk h two j) -> p qk h two j",
                                                qk=2, h=FH, two=2)
                r2v = rotr[:, 1024:2048].rearrange("p (qk h two j) -> p qk h two j",
                                                   qk=2, h=FH, two=2)
                op("dve", lambda v: v.tensor_tensor(
                    qkh[:, :, 0:32], r1v[:, 0, :, 0], r2v[:, 0, :, 1], ALU.subtract),
                   ["b:ro1", "b:ro2"], ["B:qkhat"], f"hA{c}")
                op("dve", lambda v: v.tensor_tensor(
                    qkh[:, :, 32:64], r1v[:, 0, :, 1], r2v[:, 0, :, 0], ALU.add),
                   ["b:ro1", "b:ro2"], ["B:qkhat"], f"hB{c}")
                op("dve", lambda v: v.tensor_tensor(
                    qkh[:, :, 65:97], r1v[:, 1, :, 0], r2v[:, 1, :, 1], ALU.subtract),
                   ["b:ro1", "b:ro2"], ["B:qkhat"], f"hC{c}")
                op("dve", lambda v: v.tensor_tensor(
                    qkh[:, :, 97:129], r1v[:, 1, :, 1], r2v[:, 1, :, 0], ALU.add),
                   ["b:ro1", "b:ro2"], ["B:qkhat"], f"hD{c}")
                op("dve", lambda v, so=so: v.tensor_tensor(
                    qkh[:, :, 0:64], qkh[:, :, 0:64],
                    rst[:, so:so + 8].unsqueeze(2).broadcast_to([P, 8, 64]), ALU.mult),
                   ["B:qkhat", f"b:rst{c%2}"], ["B:qkhat"], f"hRq{c}")
                op("dve", lambda v, so=so: v.tensor_tensor(
                    qkh[:, :, 65:129], qkh[:, :, 65:129],
                    rst[:, so + 8:so + 16].unsqueeze(2).broadcast_to([P, 8, 64]), ALU.mult),
                   ["B:qkhat", f"b:rst{c%2}"], ["B:qkhat"], f"hRk{c}")
                op("dve", lambda v, sqscr=sqscr: v.tensor_tensor(
                    sqscr[:, 0:512].rearrange("p (h d) -> p h d", h=FH),
                    qkh[:, :, 0:64], qkh[:, :, 65:129], ALU.mult),
                   ["B:qkhat", stok2], [stok2], f"dg{c}")
                op("dve", lambda v, sqscr=sqscr: v.reduce_sum(
                    diag[:], sqscr[:, 0:512].rearrange("p (h d) -> p h d", h=FH), axis=AX.X),
                   [stok2], ["b:diag"], f"dr{c}")
                op("dve", lambda v: v.tensor_scalar(m8[:], diag[:], b2c[:, 3:4], None,
                                                    ALU.max),
                   ["b:diag", "b:b2f"], ["b:m8"], f"dm{c}")
                op("dve", lambda v: v.tensor_scalar_mul(m8n[:], m8[:], -1.0),
                   ["b:m8"], ["b:m8n"], f"dn{c}")
                op("dve", lambda v: v.tensor_copy(qkh[:, :, 64:65], m8n[:].unsqueeze(2)),
                   ["b:m8n"], ["B:qkhat"], f"dc{c}")
                op("dve", lambda v: v.memset(qkh[:, :, 129:130], 1.0),
                   ["B:qkhat"], ["B:qkhat"], f"do{c}")
                def qtr(t):
                    last = None
                    for h in range(FH):
                        last = t.transpose(pT1[0:65, h * P:(h + 1) * P],
                                           qkhat[:, h * 130:h * 130 + 65], ident[:])
                    return last
                op("pe", qtr, ["B:qkhat", "c:ident", "p:T1"], ["p:T1"], f"qT{c}")

                def ktr2(t):
                    last = None
                    for h in range(FH):
                        last = t.transpose(pT2[0:65, h * P:(h + 1) * P],
                                           qkhat[:, h * 130 + 65:h * 130 + 130], ident[:])
                    return last
                op("pe", ktr2, ["B:qkhat", "c:ident", "p:T2"], ["p:T2"], f"kT{c}")

                def qev(v, c=c):
                    dst = qhfm.rearrange("p (h c) -> p h c", h=FH)[0:65, :, c * P:(c + 1) * P]
                    src = pT1[:].rearrange("p (h c) -> p h c", h=FH)[0:65, :, 0:P]
                    return v.tensor_copy(dst, src)
                op("dve", qev, ["p:T1"], ["B:qhfm"], f"qE{c}")

                def kev2(v, c=c, par=par, l=l):
                    dst = kxfm.rearrange("p (w h c) -> p w h c", w=4, h=FH)[0:65, l * 2 + par, :,
                                                                            c * P:(c + 1) * P]
                    src = pT2[:].rearrange("p (h c) -> p h c", h=FH)[0:65, :, 0:P]
                    return v.tensor_copy(dst, src)
                op("dve", kev2, ["p:T2"], [f"st:kx{l}{par}"], f"kE{c}")
            if p == 0 and l == 0:
                dump("x1fm", x1fm, ["B:x1fm"])
                dump("qkhat", qkhat, ["B:qkhat"])
                dump("qhfm", qhfm, ["B:qhfm"])
                dump("kxfm", kxfm[:], [f"st:kx{l}{par}"])
                dump("vfsb", vfsb[:], [f"st:vf{l}{par}"])
            w0 = (p == 0)
            for h in range(FH):
                jlist = list(range(4, 8)) if w0 else list(range(8))
                for j in jlist:
                    tc0 = max(0, j - 4)
                    tc1 = min(3, j)
                    span0, ncol = tc0 * P, (tc1 - tc0 + 1) * P
                    jp = par if j >= 4 else 1 - par
                    bk = pb[j % 6]
                    bt = f"p:b{j % 6}"

                    def smm(t, h=h, j=j, jp=jp, span0=span0, ncol=ncol, bk=bk, l=l):
                        kview = kxfm.rearrange("p (w h c) -> p w h c", w=4, h=FH)
                        return t.matmul(bk[:, span0:span0 + ncol],
                                        kview[0:65, l * 2 + jp, h, (j % 4) * P:(j % 4 + 1) * P],
                                        qhfm.rearrange("p (h c) -> p h c", h=FH)[0:65, h,
                                                                                 span0:span0 + ncol],
                                        start=True, stop=True)
                    op("pe", smm, [f"st:kx{l}{jp}", "B:qhfm", bt], [bt], f"sm{h}{j}")
                    if j <= 3:
                        tcc = j
                        op("dve", lambda v, bk=bk, tcc=tcc: v.tensor_tensor(
                            bk[:, tcc * P:(tcc + 1) * P], bk[:, tcc * P:(tcc + 1) * P],
                            trif[:], ALU.add),
                           [bt, "c:trif"], [bt], f"tf{h}{j}")
                    else:
                        tcc = j - 4
                        op("dve", lambda v, bk=bk, tcc=tcc: v.tensor_tensor(
                            bk[:, tcc * P:(tcc + 1) * P], bk[:, tcc * P:(tcc + 1) * P],
                            tric[:], ALU.add),
                           [bt, "c:tric"], [bt], f"tc{h}{j}")
                    op("act", lambda s, j=j, bk=bk, span0=span0, ncol=ncol: s.activation(
                        psb[:, j * 512 + span0:j * 512 + span0 + ncol],
                        bk[:, span0:span0 + ncol], AF.Exp),
                       [bt, "B:psb"], ["B:psb"], f"ex{h}{j}")
                ob = (h % 2) * 64
                first = True
                for j in jlist:
                    tc0 = max(0, j - 4)
                    tc1 = min(3, j)
                    span0, ncol = tc0 * P, (tc1 - tc0 + 1) * P
                    jp = par if j >= 4 else 1 - par

                    def av(t, h=h, j=j, jp=jp, span0=span0, ncol=ncol, first=first,
                           l=l, ob=ob):
                        vview = vfsb[:].rearrange("p (w ch h cc) -> p w ch h cc", w=4,
                                                  ch=4, h=FH)
                        t.matmul(pb[5][ob:ob + 64, span0:span0 + ncol],
                                 vview[:, l * 2 + jp, j % 4, h, 0:64],
                                 psb[:, j * 512 + span0:j * 512 + span0 + ncol],
                                 start=first, stop=(j == jlist[-1]))
                        return t.matmul(pb[4][0:1, span0:span0 + ncol],
                                        vview[:, l * 2 + jp, j % 4, h, 64:65],
                                        psb[:, j * 512 + span0:j * 512 + span0 + ncol],
                                        start=first, stop=(j == jlist[-1]))
                    op("pe", av, [f"st:vf{l}{jp}", "B:psb", "p:b5", "p:b4"],
                       ["p:b5", "p:b4"], f"av{h}{j}")
                    first = False
                op("dve", lambda v: v.tensor_scalar_max(rsum[0:1, :], pb[4][0:1, :], 1e-12),
                   ["p:b4"], ["b:rsum"], f"sc{h}")
                op("dve", lambda v: v.reciprocal(rsum[0:1, :], rsum[0:1, :]),
                   ["b:rsum"], ["b:rsum"], f"si2{h}")
                op("pe", lambda t: t.matmul(pb[3][:, 0:512],
                                            ones32[0:1, 0:128], rsum[0:1, :],
                                            start=True, stop=True),
                   ["c:ones32", "b:rsum", "p:b3"], ["p:b3"], f"rb{h}")
                op("dve", lambda v: v.tensor_copy(rbc[:, :], pb[3][:, :]),
                   ["p:b3", "B:rbc"], ["B:rbc"], f"re{h}")

                def odiv(v, h=h, ob=ob):
                    dst = oall[ob:ob + 64, (h // 2) * 512:(h // 2 + 1) * 512]
                    return v.tensor_tensor(dst, pb[5][ob:ob + 64, :], rbc[ob:ob + 64, :],
                                           ALU.mult)
                op("dve", odiv, ["p:b5", "B:rbc", "B:oall"], ["B:oall"], f"od{h}")
            if p == 0 and l == 0:
                dump("psb", psb, ["B:psb"])
                dump("oall", oall, ["B:oall"])
            for ec in range(8):
                sl = ec % 2
                wt = wfo_r[sl]
                wload(wt, fwo_h[(l * 8 + ec) * FH * DH:(l * 8 + ec + 1) * FH * DH, :],
                      wsc["fwo"][(l * 8 + ec) * FH * DH:(l * 8 + ec + 1) * FH * DH, :],
                      f"W:fo{sl}", f"Wsc:fo{l}{ec}", p, 4, "wfo")
                bank = pb[ec % 2]
                bt = f"p:b{ec % 2}"

                def mm(t, wt=wt, bank=bank):
                    last = None
                    for kc in range(4):
                        last = t.matmul(bank[:], wt[:, kc * P:(kc + 1) * P],
                                        oall[:, kc * 512:(kc + 1) * 512],
                                        start=(kc == 0), stop=(kc == 3))
                    return last
                op("pe", mm, [f"W:fo{sl}", "B:oall", bt], [bt], f"fo{ec}")
                op("dve", lambda v, ec=ec, bank=bank: v.tensor_copy(
                    prfm[:, ec * 512:(ec + 1) * 512], bank[:]),
                   [bt], ["B:prfm"], f"foe{ec}")
            for c in range(CPP):
                def atr(t, c=c):
                    last = None
                    for ec in range(8):
                        last = t.transpose(pT2[:, ec * P:(ec + 1) * P],
                                           prfm[:, ec * 512 + c * P:ec * 512 + (c + 1) * P],
                                           ident[:])
                    return last
                op("pe", atr, ["B:prfm", "c:ident", "p:T2"], ["p:T2"], f"bT{c}")
                op("dve", lambda v, c=c: v.tensor_tensor(
                    strm[:, c * E:(c + 1) * E], pT2[:], strm[:, c * E:(c + 1) * E], ALU.add),
                   ["p:T2", f"s:strm{c}"], [f"s:strm{c}"], f"bre{c}")

        def stageB_mlp(p, l):
            BARRIERS and rec.barrier(f"bM{p}{l}_")
            rec.fence([], WB1_TOK + WB2_TOK + ["B:psb", "B:oall", "B:rbc", "B:qkhat", "B:qhfm"], f"fM{p}{l}")
            for c in range(CPP):
                op("dve", lambda v, c=c: v.reduce_sum(
                    stats[:, c * 4:c * 4 + 2],
                    strm[:, c * E:(c + 1) * E].rearrange("p (g d) -> p g d", g=2), axis=AX.X),
                   [f"s:strm{c}"], [f"b:sst{c}"], f"ls{c}")
                op("act", lambda s, c=c: s.activation(sqs[:], strm[:, c * E:(c + 1) * E],
                                                      AF.Square),
                   [f"s:strm{c}", "b:sqs"], ["b:sqs"], f"lq{c}")
                op("dve", lambda v, c=c: v.reduce_sum(
                    stats[:, c * 4 + 2:c * 4 + 4],
                    sqs[:].rearrange("p (g d) -> p g d", g=2), axis=AX.X),
                   ["b:sqs"], [f"b:sst{c}"], f"lr{c}")
            for c in range(CPP):
                def shf(t, c=c, l=l):
                    t.matmul(pb[2][:, 0:4], shdn[:], stats[:, c * 4:(c + 1) * 4],
                             start=True, stop=False)
                    prev = carrys[:, l * 4:l * 4 + 4] if c == 0 else stats[:, (c - 1) * 4:c * 4]
                    return t.matmul(pb[2][:, 0:4], shdnb[:], prev, start=False, stop=True)
                rdep = [f"b:sst{c}", "c:shdn", "c:shdnb", "p:b2"]
                rdep += [f"st:carrys{l}"] if c == 0 else [f"b:sst{c-1}"]
                op("pe", shf, rdep, ["p:b2"], f"sh{c}")
                op("dve", lambda v, c=c: v.tensor_copy(statsb[:, c * 4:(c + 1) * 4],
                                                       pb[2][:, 0:4]),
                   ["p:b2"], [f"b:ssb{c}"], f"she{c}")
            op("dve", lambda v, l=l: v.tensor_copy(carrys[:, l * 4:l * 4 + 4], stats[:, 12:16]),
               ["b:sst3"], [f"st:carrys{l}"], "scar")
            for c in range(CPP):
                op("dve", lambda v, c=c: v.tensor_tensor(
                    stat[:, 0:2],
                    stats[:, c * 4:c * 4 + 4].rearrange("p (g d) -> p d g", g=2)[:, 0, :],
                    statsb[:, c * 4:c * 4 + 4].rearrange("p (g d) -> p d g", g=2)[:, 1, :],
                    ALU.add),
                   [f"b:sst{c}", f"b:ssb{c}"], ["b:st01"], f"ab0{c}")
                op("dve", lambda v: v.tensor_scalar_mul(stat[:, 2:4], stat[:, 0:2], 1.0 / E),
                   ["b:st01"], ["b:st23"], f"ab1{c}")
                op("dve", lambda v: v.tensor_tensor(stat[:, 4:5], stat[:, 2:3], stat[:, 2:3],
                                                    ALU.mult),
                   ["b:st23"], ["b:st45"], f"ab2{c}")
                op("dve", lambda v: v.tensor_tensor(stat[:, 5:6], stat[:, 3:4], stat[:, 4:5],
                                                    ALU.subtract),
                   ["b:st23", "b:st45"], ["b:st45"], f"ab3{c}")
                op("act", lambda s: s.activation(stat[:, 6:7], stat[:, 5:6], AF.Sqrt,
                                                 bias=epscol[:, 0:1]),
                   ["b:st45", "c:eps"], ["b:st6b"], f"ab4{c}")
                op("dve", lambda v, c=c: v.reciprocal(abcol[:, c * 4:c * 4 + 1], stat[:, 6:7]),
                   ["b:st6b"], [f"b:abc{c}"], f"ab5{c}")
                op("dve", lambda v, c=c: v.tensor_tensor(
                    stat[:, 7:8], stat[:, 2:3], abcol[:, c * 4:c * 4 + 1], ALU.mult),
                   ["b:st23", f"b:abc{c}"], ["b:st7b"], f"ab6{c}")
                op("dve", lambda v, c=c: v.tensor_scalar_mul(
                    abcol[:, c * 4 + 1:c * 4 + 2], stat[:, 7:8], -1.0),
                   ["b:st7b"], [f"b:abc{c}"], f"ab7{c}")
            for c in range(CPP):
                def shu2(t, c=c):
                    last = t.matmul(pb[2][:, 0:2], shup[:], abcol[:, c * 4:c * 4 + 2],
                                    start=True, stop=(c == 3))
                    if c < 3:
                        last = t.matmul(pb[2][:, 0:2], shupb[:],
                                        abcol[:, (c + 1) * 4:(c + 1) * 4 + 2],
                                        start=False, stop=True)
                    return last
                rdep = ["c:shup", "c:shupb", f"b:abc{c}", "p:b2"]
                if c < 3:
                    rdep.append(f"b:abc{c+1}")
                op("pe", shu2, rdep, ["p:b2"], f"su{c}")
                op("dve", lambda v, c=c: v.tensor_copy(abcol[:, c * 4 + 2:c * 4 + 4],
                                                       pb[2][:, 0:2]),
                   ["p:b2"], [f"b:abc{c}"], f"sue{c}")
            op("pe", lambda t: t.matmul(pb[3][:, 0:2], ones32[0:1, 0:128], abcol[0:1, 0:2],
                                        start=True, stop=True),
               ["c:ones32", "b:abc0", "p:b3"], ["p:b3"], "a0b")
            op("dve", lambda v: v.tensor_copy(b2c[:, 4:6], pb[3][:, 0:2]),
               ["p:b3"], ["b:a0"], "a0e")
            op("dve", lambda v, l=l: v.scalar_tensor_tensor(
                stat[:, 8:12].rearrange("p (a b) -> p a b", b=1),
                in0=carry[:, 16 + l * 4:16 + l * 4 + 4].rearrange("p (a b) -> p a b", b=1),
                scalar=b2c[:, 4:5],
                in1=b2c[:, 5:6].unsqueeze(1).broadcast_to([P, 4, 1]),
                op0=ALU.mult, op1=ALU.add),
               [f"st:carryh{l}", "b:a0"], ["b:bcol"], "bc0")
            op("dve", lambda v: v.tensor_tensor(stat[:, 8:12], stat[:, 8:12], lggc[:, 4:8],
                                                ALU.mult),
               ["b:bcol", "c:lgg"], ["b:bcol"], "bc1")
            def bcw(v):
                dst = h2fm.rearrange("p (e c) -> p e c", e=8)[:, 4:8, 4:5]
                return v.tensor_copy(dst, stat[:, 8:12].unsqueeze(2))
            op("dve", bcw, ["b:bcol", "B:h2fm"], ["B:h2fm"], "bc2")
            for c in range(CPP):
                op("dve", lambda v, c=c: v.tensor_scalar(
                    xbf[:, 0:512], strm[:, c * E:c * E + 512],
                    abcol[:, c * 4:c * 4 + 1], abcol[:, c * 4 + 1:c * 4 + 2],
                    ALU.mult, ALU.add),
                   [f"s:strm{c}", f"b:abc{c}", "b:xbf"], ["b:xbf"], f"apA{c}")
                op("dve", lambda v, c=c: v.tensor_scalar(
                    xbf[:, 512:1024], strm[:, c * E + 512:(c + 1) * E],
                    abcol[:, c * 4 + 2:c * 4 + 3], abcol[:, c * 4 + 3:c * 4 + 4],
                    ALU.mult, ALU.add),
                   [f"s:strm{c}", f"b:abc{c}", "b:xbf"], ["b:xbf"], f"apB{c}")

                def htr(t):
                    last = None
                    for z in range(8):
                        last = t.transpose(pT1[:, z * P:(z + 1) * P],
                                           xbf[:, z * P:(z + 1) * P], ident[:])
                    return last
                op("pe", htr, ["b:xbf", "c:ident", "p:T1"], ["p:T1"], f"hT{c}")
                for z in range(8):
                    colofs = (4 + c * P) if z < 4 else (5 + c * P)

                    def hev(v, z=z, colofs=colofs):
                        dst = h2fm[:, z * 520 + colofs:z * 520 + colofs + P]
                        return v.tensor_scalar_mul(dst, pT1[:, z * P:(z + 1) * P],
                                                   lggc[:, z:z + 1])
                    op("dve", hev, ["p:T1", "c:lgg", "B:h2fm"], ["B:h2fm"], f"hE{c}{z}")
            op("dve", lambda v: v.tensor_copy(xbf[:, 0:512], strm[:, 3 * E + 512:4 * E]),
               ["s:strm3", "b:xbf"], ["b:xbf"], "rcp")

            def rtr(t):
                last = None
                for z in range(4):
                    last = t.transpose(pT1[:, z * P:(z + 1) * P],
                                       xbf[:, z * P:(z + 1) * P], ident[:])
                return last
            op("pe", rtr, ["b:xbf", "c:ident", "p:T1"], ["p:T1"], "rT")

            def rcw(v, l=l):
                src = pT1[:].rearrange("p (e c) -> p e c", e=8)[:, 0:4, 127:128]
                return v.tensor_copy(carry[:, 16 + l * 4:16 + l * 4 + 4].unsqueeze(2), src)
            op("dve", rcw, ["p:T1"], [f"st:carryh{l}"], "rcw")
            if p == 0:
                dma("sync", bias1a[:, l * MSTR:(l + 1) * MSTR], fb1a_h[l * P:(l + 1) * P, :],
                    [f"c:b1a{l}"], [f"c:b1a{l}"], "b1a")
                dma("sync", bias1g[:, l * MSTR:(l + 1) * MSTR], fb1g_h[l * P:(l + 1) * P, :],
                    [f"c:b1g{l}"], [f"c:b1g{l}"], "b1g")
            for mstr in range(MSTR):
                sl = mstr % 2
                wta, wtg = w1a_r[sl], w1g_r[sl]
                base = (l * MSTR + mstr) * E
                wload(wta, fw1a_h[base:base + E, :], wsc["w1a"][base:base + E, :],
                      f"W:w1a{sl}", f"Wsc:w1a{l}{mstr}", p, 8, "w1a")
                wload(wtg, fw1g_h[base:base + E, :], wsc["w1g"][base:base + E, :],
                      f"W:w1g{sl}", f"Wsc:w1g{l}{mstr}", p, 8, "w1g")
                if p == 0:
                    for ag in range(2):
                        wt = wta if ag == 0 else wtg
                        bia = bias1a if ag == 0 else bias1g
                        btk = f"c:b1a{l}" if ag == 0 else f"c:b1g{l}"

                        def lbm(t, wt=wt):
                            last = None
                            for kc in range(8):
                                last = t.matmul(pb[4][:, 0:1], wt[:, kc * P:(kc + 1) * P],
                                                lbbc[:, kc:kc + 1], start=(kc == 0),
                                                stop=(kc == 7))
                            return last
                        op("pe", lbm, [f"W:w1a{sl}" if ag == 0 else f"W:w1g{sl}",
                                       "c:lbb", "p:b4"], ["p:b4"], f"lb{mstr}{ag}")
                        op("dve", lambda v, bia=bia, mstr=mstr: v.tensor_tensor(
                            bia[:, l * MSTR + mstr:l * MSTR + mstr + 1],
                            pb[4][:, 0:1],
                            bia[:, l * MSTR + mstr:l * MSTR + mstr + 1], ALU.add),
                           ["p:b4", btk], [btk], f"lbe{mstr}{ag}")
                for ag in range(2):
                    wt = wta if ag == 0 else wtg
                    wtk = f"W:w1a{sl}" if ag == 0 else f"W:w1g{sl}"
                    bank = pb[ag]
                    bt = f"p:b{ag}"

                    def mm(t, wt=wt, bank=bank):
                        last = None
                        for kc in range(8):
                            last = t.matmul(bank[:],
                                            wt[:, kc * P:(kc + 1) * P],
                                            h2fm.rearrange("p (e c) -> p e c", e=8)[:, kc, 4:516],
                                            start=(kc == 0), stop=(kc == 7))
                        return last
                    op("pe", mm, [wtk, "B:h2fm", bt], [bt], f"w1{mstr}{ag}")
                op("act", lambda s, mstr=mstr: s.activation(
                    sqs[:, 0:512], pb[0][:], AF.Identity,
                    bias=bias1a[:, l * MSTR + mstr:l * MSTR + mstr + 1]),
                   ["p:b0", f"c:b1a{l}", "b:sqs"], ["b:sqs"], f"ga{mstr}")
                op("act", lambda s, mstr=mstr: s.activation(
                    sqs[:, 512:1024], pb[1][:], AF.Gelu,
                    bias=bias1g[:, l * MSTR + mstr:l * MSTR + mstr + 1]),
                   ["p:b1", f"c:b1g{l}", "b:sqs"], ["b:sqs"], f"gg{mstr}")
                op("dve", lambda v, mstr=mstr: v.tensor_tensor(
                    actfm[:, mstr * 512:(mstr + 1) * 512], sqs[:, 0:512], sqs[:, 512:1024],
                    ALU.mult),
                   ["b:sqs", "B:actfm"], ["B:actfm"], f"gm{mstr}")
            if p == 0 and l == 0:
                dump("h2fm", h2fm, ["B:h2fm"])
                dump("actfm", actfm, ["B:actfm"])
            for ec in range(8):
                sl = ec % 2
                wt = w2_r[sl]
                base = (l * 8 + ec) * MSTR * P
                wload(wt, fw2_h[base:base + MSTR * P, :], wsc["w2"][base:base + MSTR * P, :],
                      f"W:w2{sl}", f"Wsc:w2{l}{ec}", p, MSTR, "w2l")
                bank = pb[2 + ec % 2]
                bt = f"p:b{2 + ec % 2}"

                def mm(t, wt=wt, bank=bank):
                    last = None
                    for kc in range(MSTR):
                        last = t.matmul(bank[:], wt[:, kc * P:(kc + 1) * P],
                                        actfm[:, kc * 512:(kc + 1) * 512],
                                        start=(kc == 0), stop=(kc == MSTR - 1))
                    return last
                op("pe", mm, [f"W:w2{sl}", "B:actfm", bt], [bt], f"w2{ec}")
                op("act", lambda s, ec=ec, bank=bank: s.activation(
                    prfm[:, ec * 512:(ec + 1) * 512], bank[:], AF.Identity,
                    bias=bias2[:, l * 8 + ec:l * 8 + ec + 1]),
                   [bt, f"c:b2_{l}", "B:prfm"], ["B:prfm"], f"w2e{ec}")
            for c in range(CPP):
                def atr(t, c=c):
                    last = None
                    for ec in range(8):
                        last = t.transpose(pT2[:, ec * P:(ec + 1) * P],
                                           prfm[:, ec * 512 + c * P:ec * 512 + (c + 1) * P],
                                           ident[:])
                    return last
                op("pe", atr, ["B:prfm", "c:ident", "p:T2"], ["p:T2"], f"wT{c}")
                op("dve", lambda v, c=c: v.tensor_tensor(
                    strm[:, c * E:(c + 1) * E], pT2[:], strm[:, c * E:(c + 1) * E], ALU.add),
                   ["p:T2", f"s:strm{c}"], [f"s:strm{c}"], f"wre{c}")

        def stageF(p):
            BARRIERS and rec.barrier(f"bF{p}_")
            rec.fence([], ALIAS_TOK, f"fF{p}")
            x1rl = rotr
            for c in range(CPP):
                gc = p * CPP + c
                dma("sync", x1rl[:, 0:1024], x1sp_h[gc * P:(gc + 1) * P, :],
                    [f"sp:{gc}", "b:x1rl"], ["b:x1rl"], "x1rl")
                op("dve", lambda v, c=c: v.scalar_tensor_tensor(
                    strm[:, c * E:(c + 1) * E], in0=x1rl[:, 0:1024], scalar=1.0,
                    in1=strm[:, c * E:(c + 1) * E], op0=ALU.mult, op1=ALU.add,
                    accum_out=stat[:, 0:1]),
                   ["b:x1rl", f"s:strm{c}"], [f"s:strm{c}", "b:st0"], f"fr{c}")
                emit_ln(c, f"lnF_{p}_{c}")

        def store_out(p):
            for c in range(CPP):
                gc = p * CPP + c
                dma("sync", out_h[gc * P:(gc + 1) * P, :], strm[:, c * E:(c + 1) * E],
                    [f"s:strm{c}"], [f"o:{gc}"], "ost")

        for p in range(NPASS):
            stageA(p)
            if "B0a" in stages or "B0" in stages:
                stageB_attn(p, 0)
            if "B0" in stages:
                stageB_mlp(p, 0)
            if "B1" in stages:
                stageB_attn(p, 1)
                stageB_mlp(p, 1)
            if "F" in stages:
                stageF(p)
            store_out(p)

        rec.add("sync", lambda q: q.nop(),
                reads=[f"o:{gc}" for gc in range(NPASS * CPP)] + dump_done,
                writes=[], tag="fin")

        engmap = {"pe": block.tensor, "act": block.scalar, "dve": block.vector,
                  "sync": block.sync, "gps": block.gpsimd}
        rec.emit(sems, engmap)

    return nc, cns


def kernel(**inputs):
    nc, cns = build_program()
    in_maps = [make_core_inputs(inputs, core, cns) for core in range(8)]
    res = run_bass_kernel_spmd(nc, in_maps, core_ids=list(range(8)))
    out = np.stack([res.results[i]["out"] for i in range(8)], axis=0)
    return np.ascontiguousarray(out.astype(np.float32))
